# version 11
# baseline (speedup 1.0000x reference)
import numpy as np
import ml_dtypes
import concourse.bass as bass
import concourse.mybir as mybir
from concourse.bass_utils import run_bass_kernel_spmd
from contextlib import ExitStack

F32 = mybir.dt.float32
BF16 = mybir.dt.bfloat16
AF = mybir.ActivationFunctionType
ALU = mybir.AluOpType
AX = mybir.AxisListType.X

NCORES = 8
M = 2048
D = 1024
H = 16
DH = 64
CH = 128
NCH = M // CH
SCALE = float(DH ** (-0.25))
E = 8
GC = 1.0 / float(np.e)
GL = 0.5 - float(np.log(4.0 / 3.0))
GU = 0.5

ENGINES = ("sync", "gpsimd", "vector", "scalar", "tensor")


class Buf:
    __slots__ = ("writer", "readers", "name")

    def __init__(self, name=""):
        self.writer = None
        self.readers = []
        self.name = name


class PB:
    def __init__(self):
        self.ops = {e: [] for e in ENGINES}
        self.floor = []
        self.gseq = 0

    def op(self, eng, fn, reads=(), writes=(), deps=(), fence=False, dma=False,
           is_async=False, grp=None):
        if eng == "vector":
            fence = True
        dd = list(deps)
        for b in reads:
            if b.writer is not None:
                dd.append(b.writer)
        for b in writes:
            if b.writer is not None:
                dd.append(b.writer)
            dd.extend(b.readers)
        dd.extend(self.floor)
        idx = len(self.ops[eng])
        self.gseq += 1
        self.ops[eng].append({"fn": fn, "deps": dd, "dma": dma, "fence": fence,
                              "async": is_async or dma, "mark": False,
                              "wkey": grp or tuple(id(b) for b in writes),
                              "gseq": self.gseq})
        tok = (eng, idx)
        for b in reads:
            b.readers.append(tok)
        for b in writes:
            b.readers = []
            b.writer = tok
        return tok

    def barrier(self):
        self.floor = [(e, len(self.ops[e]) - 1) for e in ENGINES if self.ops[e]]

    def schedule(self):
        EST = {"sync": 800, "gpsimd": 1500, "vector": 500, "scalar": 600,
               "tensor": 220}
        XLAT = 250
        nodes = []
        idmap = {}
        for e in ENGINES:
            i = 0
            n = len(self.ops[e])
            while i < n:
                run = [i]
                if e == "tensor" and self.ops[e][i]["wkey"]:
                    k = self.ops[e][i]["wkey"]
                    j = i + 1
                    while (j < n and self.ops[e][j]["wkey"] == k
                           and self.ops[e][j]["gseq"] == self.ops[e][j - 1]["gseq"] + 1):
                        run.append(j)
                        j += 1
                ni = len(nodes)
                nodes.append((e, run))
                for idx0 in run:
                    idmap[(e, idx0)] = ni
                i = run[-1] + 1
        ndeps = [[] for _ in nodes]
        succs = [[] for _ in nodes]
        indeg = [0] * len(nodes)
        for e in ENGINES:
            for i, o in enumerate(self.ops[e]):
                ni = idmap[(e, i)]
                for d in o["deps"]:
                    nd = idmap[d]
                    if nd == ni:
                        continue
                    ndeps[ni].append(nd)
        for ni in range(len(nodes)):
            ndeps[ni] = sorted(set(ndeps[ni]))
            for nd in ndeps[ni]:
                succs[nd].append(ni)
            indeg[ni] = len(ndeps[ni])
        prio = [0] * len(nodes)
        dur_n = [EST[nodes[ni][0]] * len(nodes[ni][1]) for ni in range(len(nodes))]
        topo = []
        tmp_in = indeg[:]
        import collections
        dq = collections.deque(ni for ni in range(len(nodes)) if tmp_in[ni] == 0)
        while dq:
            ni = dq.popleft()
            topo.append(ni)
            for s in succs[ni]:
                tmp_in[s] -= 1
                if tmp_in[s] == 0:
                    dq.append(s)
        assert len(topo) == len(nodes), "cycle in dep graph"
        for ni in reversed(topo):
            e = nodes[ni][0]
            best = 0
            for s in succs[ni]:
                lat = XLAT if nodes[s][0] != e else 0
                if prio[s] + lat > best:
                    best = prio[s] + lat
            prio[ni] = best + dur_n[ni]
        import os
        keep = set((os.environ.get("KKEEP") or "").split(","))
        import heapq
        ready = {e: [] for e in ENGINES}
        unmet = indeg[:]
        finish = [0.0] * len(nodes)
        for ni in range(len(nodes)):
            if unmet[ni] == 0:
                e0 = nodes[ni][0]
                key = (ni if e0 in keep else -prio[ni])
                heapq.heappush(ready[e0], (key, ni))
        order = {e: [] for e in ENGINES}
        etime = {e: 0.0 for e in ENGINES}
        import itertools
        n_done = 0
        while n_done < len(nodes):
            cand = [e for e in ENGINES if ready[e]]
            if not cand:
                raise RuntimeError("scheduler deadlock")
            e = min(cand, key=lambda x: etime[x])
            _, ni = heapq.heappop(ready[e])
            t0 = etime[e]
            for nd in ndeps[ni]:
                de = nodes[nd][0]
                lat = XLAT if de != e else 0
                if finish[nd] + lat > t0:
                    t0 = finish[nd] + lat
            t1 = t0 + dur_n[ni]
            etime[e] = t1
            finish[ni] = t1
            order[e].extend(nodes[ni][1])
            n_done += 1
            for s in succs[ni]:
                unmet[s] -= 1
                if unmet[s] == 0:
                    heapq.heappush(ready[nodes[s][0]], (-prio[s], s))
        import os
        ksch = set((os.environ.get("KSCHED") or "sync,gpsimd,vector,scalar,tensor").split(","))
        for e in ENGINES:
            if e not in ksch:
                order[e] = list(range(len(self.ops[e])))
        remap = {}
        newops = {}
        for e in ENGINES:
            newops[e] = [self.ops[e][i] for i in order[e]]
            for newi, oldi in enumerate(order[e]):
                remap[(e, oldi)] = (e, newi)
        for e in ENGINES:
            for o in newops[e]:
                o["deps"] = [remap[d] for d in o["deps"]]
        self.ops = newops

    def finalize_and_emit(self, nc, ctx):
        import os
        if not os.environ.get("KNOSCHED"):
            self.schedule()
        for e in ENGINES:
            for i, o in enumerate(self.ops[e]):
                seen = {}
                for (de, di) in o["deps"]:
                    if de == e and di >= i:
                        continue
                    if (de == e and not o["fence"]
                            and not self.ops[de][di]["async"]):
                        continue
                    if seen.get(de, -1) < di:
                        seen[de] = di
                o["waits"] = seen
                for de, di in seen.items():
                    self.ops[de][di]["mark"] = True
        for e in ENGINES:
            for o in self.ops[e]:
                if o["dma"]:
                    o["mark"] = True
        semval = {e: [] for e in ENGINES}
        for e in ENGINES:
            v = 0
            for o in self.ops[e]:
                if o["mark"]:
                    v += 16 if o["dma"] else 1
                semval[e].append(v)
        sems = {e: ctx.enter_context(nc.semaphore("sem_" + e)) for e in ENGINES}
        block = ctx.enter_context(nc.Block())

        def make_section(ename):
            def section(eng):
                water = {e: 0 for e in ENGINES}
                for o in self.ops[ename]:
                    for de, di in sorted(o["waits"].items()):
                        v = semval[de][di]
                        if v > water[de]:
                            eng.wait_ge(sems[de], v)
                            water[de] = v
                    inst = o["fn"](eng)
                    if o["mark"]:
                        inst.then_inc(sems[ename], 16 if o["dma"] else 1)
            return section

        block.sync(make_section("sync"))
        block.gpsimd(make_section("gpsimd"))
        block.vector(make_section("vector"))
        block.scalar(make_section("scalar"))
        block.tensor(make_section("tensor"))


def build_program(debug=False):
    nc = bass.Bass("TRN2", target_bir_lowering=False, debug=False)

    x_in = nc.dram_tensor("x", [M, D], F32, kind="ExternalInput").ap()
    wqkv = nc.dram_tensor("wqkv", [D, 3 * D], F32, kind="ExternalInput").ap()
    wout = nc.dram_tensor("wout", [D, D], F32, kind="ExternalInput").ap()
    wg_h = nc.dram_tensor("wg", [E, D, D], F32, kind="ExternalInput").ap()
    wu_h = nc.dram_tensor("wu", [E, D, D], F32, kind="ExternalInput").ap()
    wd_h = nc.dram_tensor("wd", [E, D, D], F32, kind="ExternalInput").ap()
    wr_h = nc.dram_tensor("wr", [D, E], F32, kind="ExternalInput").ap()
    temp_h = nc.dram_tensor("temp1", [128, 1], F32, kind="ExternalInput").ap()
    cos_h = nc.dram_tensor("cosr", [M, 32], BF16, kind="ExternalInput").ap()
    sin_h = nc.dram_tensor("sinr", [M, 32], BF16, kind="ExternalInput").ap()
    identb_h = nc.dram_tensor("identb", [128, 128], BF16, kind="ExternalInput").ap()
    identf_h = nc.dram_tensor("identf", [128, 128], F32, kind="ExternalInput").ap()
    masku4_h = nc.dram_tensor("masku4", [128, 512], BF16, kind="ExternalInput").ap()
    onesb_h = nc.dram_tensor("onesb", [128, 1], BF16, kind="ExternalInput").ap()
    esel4_h = nc.dram_tensor("esel4", [4, 256], BF16, kind="ExternalInput").ap()
    e8sel_h = nc.dram_tensor("e8sel", [8, 1024], BF16, kind="ExternalInput").ap()
    prefm_h = nc.dram_tensor("prefmask", [64, 8], F32, kind="ExternalInput").ap()
    consts_h = nc.dram_tensor("consts", [128, 8], F32, kind="ExternalInput").ap()
    out_h = nc.dram_tensor("out", [M, D], F32, kind="ExternalOutput").ap()

    dbg = {}
    if debug:
        for nm, shape in [("d_attn", [64, 2048]), ("d_x1", [128, 1024]),
                          ("d_wfin", [128, 8]), ("d_kv", [64, 1024]),
                          ("d_ks", [64, 16])]:
            dbg[nm] = nc.dram_tensor(nm, shape, F32, kind="ExternalOutput").ap()

    kv_loc = nc.dram_tensor("kv_loc", [64, 1040], BF16, kind="Internal")
    kv_all = nc.dram_tensor("kv_all", [NCORES * 64, 1040], BF16, kind="Internal")

    pb = PB()
    ctx = ExitStack()
    out_toks = []
    OP = pb.op

    sb = lambda name, shape, dt: ctx.enter_context(nc.sbuf_tensor(name, shape, dt))

    x_bf = sb("x_bf", [128, NCH * D], BF16)
    identb = sb("s_identb", [128, 128], BF16)
    identf = sb("s_identf", [128, 128], F32)
    masku4 = sb("s_masku4", [128, 512], BF16)
    cos_sb = sb("cos_sb", [128, NCH * 32], BF16)
    sin_sb = sb("sin_sb", [128, NCH * 32], BF16)
    onesb = sb("s_onesb", [128, 1], BF16)
    esel4 = sb("s_esel4", [4, 256], BF16)
    e8sel = sb("s_e8sel", [8, 1024], BF16)
    prefm = sb("prefm", [64, 8], F32)
    temp_sb = sb("temp_sb", [128, 1], F32)
    csts = sb("csts", [128, 8], F32)
    C_INVD = lambda: csts[:, 0:1]
    C_EPS = lambda: csts[:, 1:2]
    C_SCALE = lambda: csts[:, 2:3]
    C_N10 = lambda: csts[:, 3:4]
    C_N103 = lambda: csts[:, 4:5]
    KV32 = sb("KV32", [64, 1024], F32)
    ks32 = sb("ks32", [64, 16], F32)
    KVbf = sb("KVbf", [128, 1024], BF16)
    ksbf = sb("ksbf", [128, 16], BF16)
    _rpa = sb("rpas", [128, D], F32)
    _lncen = sb("ln_cen", [128, D], F32)
    _lnm = sb("ln_m", [128, 1], F32)
    _lnss = sb("ln_ss", [128, 1], F32)
    _lnrs = sb("ln_rs", [128, 1], F32)
    _lnxnb = sb("xn_bfs", [128, D], BF16)
    _bln = Buf("ln")
    _bxnb = Buf("xnb")
    lnP = []
    for p in range(2):
        d = dict(cen=_lncen, sq=_rpa, m=_lnm, ss=_lnss, rs=_lnrs, xnb=_lnxnb,
                 xnfm=sb("xn_fm%d" % p, [128, D], BF16))
        d["b"] = _bln
        d["b_xnb"] = _bxnb
        d["b_xnfm"] = Buf("xnfm%d" % p)
        lnP.append(d)
    _z32 = sb("z32s", [128, D], F32)
    _rpb = sb("rpbs", [128, D], BF16)
    _zbf = sb("zbfs", [128, D], BF16)
    z32P = [_z32, _z32]
    rpaP = [_rpa, _rpa]
    rpbP = [_rpb, _rpb]
    zbfP = [_zbf, _zbf]
    _bz = Buf("z32s")
    _brp = Buf("rps")
    _bzb = Buf("zbfs")
    b_z32 = [_bz, _bz]
    b_rp = [_brp, _brp]
    b_zbf = [_bzb, _bzb]

    b_x = [Buf("x%d" % c) for c in range(NCH)]
    b_const = Buf("const")
    b_kv32 = Buf("kv32")
    b_ks32 = Buf("ks32")
    b_kvbf = Buf("kvbf")
    b_ksbf = Buf("ksbf")

    def dma(eng_name, out_ap, in_ap, reads=(), writes=(), deps=(), fence=False):
        return OP(eng_name, lambda e, o=out_ap, i=in_ap: e.dma_start(out=o, in_=i),
                  reads=reads, writes=writes, deps=deps, fence=fence, dma=True)

    dma("sync", identb[:, :], identb_h, writes=[b_const])
    dma("sync", identf[:, :], identf_h, writes=[b_const])
    dma("sync", masku4[:, :], masku4_h, writes=[b_const])
    dma("sync", cos_sb[:, :].rearrange("p (c j) -> p c j", j=32),
        cos_h.rearrange("(c p) j -> p c j", p=128), writes=[b_const])
    dma("sync", sin_sb[:, :].rearrange("p (c j) -> p c j", j=32),
        sin_h.rearrange("(c p) j -> p c j", p=128), writes=[b_const])
    dma("sync", onesb[:, :], onesb_h, writes=[b_const])
    dma("sync", esel4[:, :], esel4_h, writes=[b_const])
    dma("sync", e8sel[:, :], e8sel_h, writes=[b_const])
    dma("sync", prefm[:, :], prefm_h, writes=[b_const])
    dma("sync", temp_sb[:, :], temp_h, writes=[b_const])
    dma("sync", csts[:, :], consts_h, writes=[b_const])
    OP("vector", lambda e: e.memset(KVbf[:, :], 0.0), writes=[b_kvbf])
    OP("vector", lambda e: e.memset(ksbf[:, :], 0.0), writes=[b_ksbf])
    dma("gpsimd", x_bf[:, :].rearrange("p (c d) -> p c d", d=D),
        x_in.rearrange("(c p) d -> p c d", p=128), writes=b_x)

    def emit_ln(c, ps_tp, b_pstp, lnset=None):
        L = (lnset or lnP)[c % 2]
        xt = x_bf[:, D * c:D * (c + 1)]
        OP("vector", lambda e: e.reduce_sum(L["m"][:, :], xt, axis=AX),
           reads=[b_x[c]], writes=[L["b"]])
        OP("vector", lambda e: e.tensor_scalar(out=L["m"][:, :], in0=L["m"][:, :],
                                               scalar1=-1.0 / D, scalar2=None, op0=ALU.mult),
           reads=[L["b"]], writes=[L["b"]])
        OP("scalar", lambda e: e.activation(out=L["cen"][:, :], in_=xt,
                                            func=AF.Identity, bias=L["m"][:, :]),
           reads=[b_x[c], L["b"]], writes=[L["b"]])
        OP("scalar", lambda e: e.activation(out=L["sq"][:, :], in_=L["cen"][:, :],
                                            func=AF.Square, accum_out=L["ss"][:, :]),
           reads=[L["b"]], writes=[L["b"], b_rp[0]])
        OP("scalar", lambda e: e.activation(out=L["rs"][:, :], in_=L["ss"][:, :],
                                            func=AF.Sqrt, scale=C_INVD(), bias=C_EPS()),
           reads=[L["b"], b_const], writes=[L["b"]], fence=True)
        OP("vector", lambda e: e.reciprocal(L["rs"][:, :], L["rs"][:, :]),
           reads=[L["b"]], writes=[L["b"]])
        OP("vector", lambda e: e.tensor_scalar(out=L["xnb"][:, :], in0=L["cen"][:, :],
                                               scalar1=L["rs"][:, :], scalar2=None, op0=ALU.mult),
           reads=[L["b"]], writes=[L["b_xnb"]])
        for dc in range(8):
            OP("tensor", lambda e, dc=dc: e.transpose(
                ps_tp[:, 128 * dc:128 * (dc + 1)], L["xnb"][:, 128 * dc:128 * (dc + 1)],
                identb[:, :]),
               reads=[L["b_xnb"], b_const], writes=[b_pstp])
        OP("vector", lambda e: e.tensor_copy(out=L["xnfm"][:, :], in_=ps_tp[:, 0:D]),
           reads=[b_pstp], writes=[L["b_xnfm"]])
        return L

    def emit_ropefeat(c, zbf_out, b_out, rope=None):
        p = c % 2
        z32, za, zb, b_z, b_s = (rope or (z32P[p], rpaP[p], rpbP[p],
                                          b_z32[p], b_rp[p]))

        def v3(t, off):
            return t[:, :].rearrange("p (h d) -> p h d", d=DH)[:, :, off:off + 32]

        cs = cos_sb[:, :].rearrange("p (cc j) -> p cc j", j=32)[:, c:c + 1, :]
        sn = sin_sb[:, :].rearrange("p (cc j) -> p cc j", j=32)[:, c:c + 1, :]
        za3 = za[:, 0:512].rearrange("p (h j) -> p h j", j=32)
        zb3 = zb[:, 0:512].rearrange("p (h j) -> p h j", j=32)

        def tt3(e, out_ap, a_ap, b_ap, op):
            aa, bb = bass.broadcast_tensor_aps(a_ap, b_ap)
            return e.tensor_tensor(out=out_ap, in0=aa, in1=bb, op=op)

        q1 = v3(z32, 0)
        q2 = v3(z32, 32)
        OP("vector", lambda e: tt3(e, za3, q1, cs, ALU.mult),
           reads=[b_z, b_const], writes=[b_s])
        OP("vector", lambda e: tt3(e, zb3, q2, sn, ALU.mult),
           reads=[b_z, b_const], writes=[b_s])
        OP("vector", lambda e: e.tensor_tensor(out=za[:, 0:512], in0=za[:, 0:512],
                                               in1=zb[:, 0:512], op=ALU.subtract),
           reads=[b_s], writes=[b_s])
        OP("vector", lambda e: tt3(e, zb3, q1, sn, ALU.mult),
           reads=[b_z, b_const], writes=[b_s])
        OP("vector", lambda e: tt3(e, v3(z32, 32), v3(z32, 32), cs, ALU.mult),
           reads=[b_z, b_const], writes=[b_z])
        OP("vector", lambda e: e.tensor_tensor(
            out=v3(z32, 32), in0=v3(z32, 32), in1=zb3, op=ALU.add),
           reads=[b_z, b_s], writes=[b_z])
        OP("vector", lambda e: e.tensor_copy(out=v3(z32, 0), in_=za3),
           reads=[b_s], writes=[b_z])
        OP("vector", lambda e: e.tensor_scalar(out=za[:, 0:D], in0=z32[:, :],
                                               scalar1=0.0, scalar2=None, op0=ALU.min),
           reads=[b_z], writes=[b_s])
        OP("scalar", lambda e: e.activation(out=zb[:, 0:D], in_=za[:, 0:D],
                                            func=AF.Exp, scale=C_SCALE()),
           reads=[b_s, b_const], writes=[b_s])
        OP("scalar", lambda e: e.activation(out=za[:, 0:D], in_=z32[:, :],
                                            func=AF.Relu, scale=C_SCALE()),
           reads=[b_z, b_const], writes=[b_s])
        OP("vector", lambda e: e.tensor_tensor(out=zbf_out, in0=za[:, 0:D],
                                               in1=zb[:, 0:D], op=ALU.add),
           reads=[b_s], writes=[b_out])

    p12 = ExitStack()
    k_tm = p12.enter_context(nc.sbuf_tensor("k_tm", [128, NCH * D], BF16))
    v_tm = p12.enter_context(nc.sbuf_tensor("v_tm", [128, NCH * D], BF16))
    b_ktm = [Buf("ktm%d" % c) for c in range(NCH)]
    b_vtm = [Buf("vtm%d" % c) for c in range(NCH)]

    with ExitStack() as p1:
        sb1 = lambda name, shape, dt: p1.enter_context(nc.sbuf_tensor(name, shape, dt))
        ps1 = lambda name, shape, dt: p1.enter_context(nc.psum_tensor(name, shape, dt))
        wkv_bf = sb1("wkv_bf", [128, 8 * 2048], BF16)
        l1cen = sb1("p1cen", [128, D], F32)
        l1sq = sb1("p1sq", [128, D], F32)
        l1m = sb1("p1m", [128, 1], F32)
        l1ss = sb1("p1ss", [128, 1], F32)
        l1rs = sb1("p1rs", [128, 1], F32)
        _b1ln = Buf("p1ln")
        ln1P = []
        for p in range(2):
            d = dict(cen=l1cen, sq=l1sq, m=l1m, ss=l1ss, rs=l1rs,
                     xnb=sb1("p1xnb%d" % p, [128, D], BF16),
                     xnfm=sb1("p1xnfm%d" % p, [128, D], BF16))
            d["b"] = _b1ln
            d["b_xnb"] = Buf("p1xnb%d" % p)
            d["b_xnfm"] = Buf("p1xnfm%d" % p)
            ln1P.append(d)
        rope1P = []
        for p in range(2):
            rope1P.append((sb1("p1z32%d" % p, [128, D], F32),
                           sb1("p1rpa%d" % p, [128, D], F32),
                           sb1("p1rpb%d" % p, [128, D], BF16),
                           Buf("p1z%d" % p), Buf("p1rp%d" % p)))
        ps_tp1 = ps1("ps_tp1", [128, 1024], BF16)
        ps_kvpP = [ps1("ps_kvp%d" % p, [128, 1024], F32) for p in range(2)]
        kv16 = sb1("kv16", [64, 1040], BF16)
        ps_kv1 = ps1("ps_kv1", [64, 1024], F32)
        ps_ks1 = ps1("ps_ks1", [64, 16], F32)
        b_wkv = Buf("wkv")
        b_tp1 = Buf("tp1")
        b_kvp = [Buf("kvp%d" % p) for p in range(2)]
        b_pkv = Buf("pkv")
        b_kv16 = Buf("kv16")
        b_pks = Buf("pks")

        dma("gpsimd", wkv_bf[:, :].rearrange("p (dc n) -> p dc n", n=2048),
            wqkv.rearrange("(dc p) n -> p dc n", p=128)[:, :, D:3 * D],
            writes=[b_wkv])

        for c in range(NCH):
            par = c % 2
            L = emit_ln(c, ps_tp1, b_tp1, lnset=ln1P)
            ps_kvp = ps_kvpP[par]
            b_kp = b_kvp[par]
            for which in ("k", "v"):
                base = 0 if which == "k" else 1024
                for j in range(2):
                    for dc in range(8):
                        OP("tensor", lambda e, j=j, dc=dc, base=base, L=L, ps_kvp=ps_kvp: e.matmul(
                            ps_kvp[:, 512 * j:512 * (j + 1)],
                            L["xnfm"][:, 128 * dc:128 * (dc + 1)],
                            wkv_bf[:, 2048 * dc + base + 512 * j: 2048 * dc + base + 512 * (j + 1)],
                            start=(dc == 0), stop=(dc == 7)),
                           reads=[L["b_xnfm"], b_wkv], writes=[b_kp])
                if which == "k":
                    OP("scalar", lambda e, par=par, ps_kvp=ps_kvp: e.copy(
                        out=rope1P[par][0][:, :], in_=ps_kvp[:, 0:D]),
                       reads=[b_kp], writes=[rope1P[par][3]])
                else:
                    OP("vector", lambda e, c=c, ps_kvp=ps_kvp: e.tensor_copy(
                        out=v_tm[:, D * c:D * (c + 1)], in_=ps_kvp[:, 0:D]),
                       reads=[b_kp], writes=[b_vtm[c]])
            emit_ropefeat(c, k_tm[:, D * c:D * (c + 1)], b_ktm[c],
                          rope=rope1P[par])
            for h in range(H):
                OP("tensor", lambda e, c=c, h=h: e.matmul(
                    ps_kv1[:, 64 * h:64 * (h + 1)],
                    k_tm[:, D * c + 64 * h: D * c + 64 * (h + 1)],
                    v_tm[:, D * c + 64 * h: D * c + 64 * (h + 1)],
                    start=(c == 0 and (h == 0 or h == 8)), stop=(c == NCH - 1)),
                   reads=[b_ktm[c], b_vtm[c]], writes=[b_pkv])
            for h in range(H):
                OP("tensor", lambda e, c=c, h=h: e.matmul(
                    ps_ks1[:, h:h + 1],
                    k_tm[:, D * c + 64 * h: D * c + 64 * (h + 1)],
                    onesb[:, :],
                    start=(c == 0 and h == 0), stop=(c == NCH - 1)),
                   reads=[b_ktm[c], b_const], writes=[b_pks])

        OP("scalar", lambda e: e.copy(out=kv16[:, 0:1024], in_=ps_kv1[:, :]),
           reads=[b_pkv], writes=[b_kv16])
        OP("scalar", lambda e: e.copy(out=kv16[:, 1024:1040], in_=ps_ks1[:, :]),
           reads=[b_pks], writes=[b_kv16])
        t1 = dma("sync", kv_loc.ap()[:, :], kv16[:, :], reads=[b_kv16])
        for c0 in (0, 1):
            emit_ln(c0, ps_tp1, b_tp1)
        pb.barrier()

    with ExitStack() as pex:
        slabs = pex.enter_context(nc.sbuf_tensor("slabs", [64, 8 * 1040], BF16))
        slab32 = pex.enter_context(nc.sbuf_tensor("slab32", [64, 1040], F32))
        kvoff = pex.enter_context(nc.sbuf_tensor("kvoff", [64, 1040], F32))
        tok_cc = OP("gpsimd", lambda e: e.collective_compute(
            "AllGather", ALU.bypass, replica_groups=[list(range(NCORES))],
            ins=[kv_loc.ap()], outs=[kv_all.ap()]),
            deps=[t1], is_async=True)
        b_slab = Buf("slabs")
        b_s32 = Buf("slab32")
        b_kvoff = Buf("kvoff")
        dma("gpsimd", slabs[:, :].rearrange("p (r n) -> p r n", n=1040),
            kv_all.ap().rearrange("(r p) n -> p r n", p=64),
            deps=[tok_cc], writes=[b_slab])
        for j in range(NCORES):
            OP("vector", lambda e, j=j: e.tensor_copy(
                out=slab32[:, :], in_=slabs[:, 1040 * j:1040 * (j + 1)]),
               reads=[b_slab], writes=[b_s32])
            if j == 0:
                OP("vector", lambda e: e.tensor_scalar(
                    out=kvoff[:, :], in0=slab32[:, :],
                    scalar1=prefm[:, 0:1], scalar2=None, op0=ALU.mult),
                   reads=[b_s32, b_const], writes=[b_kvoff])
            else:
                OP("vector", lambda e, j=j: e.scalar_tensor_tensor(
                    out=kvoff[:, :], in0=slab32[:, :],
                    scalar=prefm[:, j:j + 1], in1=kvoff[:, :],
                    op0=ALU.mult, op1=ALU.add),
                   reads=[b_s32, b_const, b_kvoff], writes=[b_kvoff])
        OP("vector", lambda e: e.tensor_copy(out=KV32[:, :], in_=kvoff[:, 0:1024]),
           reads=[b_kvoff], writes=[b_kv32])
        OP("vector", lambda e: e.tensor_copy(out=ks32[:, :], in_=kvoff[:, 1024:1040]),
           reads=[b_kvoff], writes=[b_ks32])
        OP("vector", lambda e: e.tensor_copy(out=KVbf[0:64, :], in_=KV32[:, :]),
           reads=[b_kv32], writes=[b_kvbf])
        OP("vector", lambda e: e.tensor_copy(out=ksbf[0:64, :], in_=ks32[:, :]),
           reads=[b_ks32], writes=[b_ksbf])
        if debug:
            dma("sync", dbg["d_kv"], KV32[:, :], reads=[b_kv32])
            dma("sync", dbg["d_ks"], ks32[:, :], reads=[b_ks32])
        pb.barrier()

    with ExitStack() as p2:
        sb2 = lambda name, shape, dt: p2.enter_context(nc.sbuf_tensor(name, shape, dt))
        ps2 = lambda name, shape, dt: p2.enter_context(nc.psum_tensor(name, shape, dt))
        wq_bf = sb2("wq_bf", [128, 8 * 1024], BF16)
        wo_bf = sb2("wo_bf", [128, 16 * 1024], BF16)
        qfmP = [sb2("qfm%d" % p, [128, H * 128], BF16) for p in range(2)]
        _kfm = sb2("kfms", [64, H * 128], BF16)
        kfmP = [_kfm, _kfm]
        smask = sb2("smask", [128, 512], BF16)
        denr = sb2("denr", [128, 16], F32)
        denA = sb2("denA", [128, 4], F32)
        rT = sb2("rT", [4, 128], BF16)
        divs = sb2("divs", [64, 512], F32)
        attn_c = sb2("attn_c", [128, H * 128], BF16)
        o1_bf = sb2("o1_bf", [128, D], BF16)

        ps_tp2P = [ps2("ps_tp2%d" % p, [128, 1024], BF16) for p in range(2)]
        ps_big = ps2("ps_big", [128, 512], F32)
        ps_S = ps2("ps_S", [128, 512], F32)
        ps_dr = ps2("ps_dr", [128, 256], F32)
        ps_OP = [ps2("ps_O%d" % p, [64, 512], F32) for p in range(2)]
        ps_o1 = ps2("ps_o1", [128, 512], F32)

        b_wq = Buf("wq")
        b_wo = Buf("wo")
        b_qfm = [Buf("qfm%d" % p) for p in range(2)]
        _bkfm = Buf("kfms")
        b_kfm = [_bkfm, _bkfm]
        b_sm = Buf("smask")
        b_denr = Buf("denr")
        b_rT = Buf("rT")
        b_div = Buf("divs")
        b_attn = Buf("attn")
        b_o1 = Buf("o1")
        b_tp2 = [Buf("tp2%d" % p) for p in range(2)]
        b_big = Buf("big")
        b_psS = Buf("psS")
        b_pdr = Buf("pdr")
        b_psOP = [Buf("psO0"), Buf("psO1")]
        b_po1 = Buf("po1")

        OP("vector", lambda e: e.memset(qfmP[0][:, :], 0.0), writes=[b_qfm[0]])
        OP("vector", lambda e: e.memset(qfmP[1][:, :], 0.0), writes=[b_qfm[1]])
        OP("vector", lambda e: e.memset(attn_c[:, :], 0.0), writes=[b_attn])
        OP("vector", lambda e: e.memset(denr[:, :], 0.0), writes=[b_denr])
        OP("vector", lambda e: e.memset(wo_bf[:, :], 0.0), writes=[b_wo])
        dma("gpsimd", wq_bf[:, :].rearrange("p (dc n) -> p dc n", n=1024),
            wqkv.rearrange("(dc p) n -> p dc n", p=128)[:, :, 0:D], writes=[b_wq])
        dma("gpsimd", wo_bf[0:64, :].rearrange("p (h n) -> p h n", n=1024),
            wout.rearrange("(h p) n -> p h n", p=64), writes=[b_wo])

        for c in range(NCH):
            par = c % 2
            qfm = qfmP[par]
            kfm = kfmP[par]
            bq = b_qfm[par]
            bk = b_kfm[par]
            ps_tp2 = ps_tp2P[par]
            btp = b_tp2[par]
            L = lnP[par] if c < 2 else emit_ln(c, ps_tp2, btp)
            for j in range(2):
                for dc in range(8):
                    OP("tensor", lambda e, j=j, dc=dc, L=L: e.matmul(
                        ps_big[:, :],
                        L["xnfm"][:, 128 * dc:128 * (dc + 1)],
                        wq_bf[:, 1024 * dc + 512 * j: 1024 * dc + 512 * (j + 1)],
                        start=(dc == 0), stop=(dc == 7)),
                       reads=[L["b_xnfm"], b_wq], writes=[b_big])
                OP("scalar", lambda e, par=par, j=j: e.copy(
                    out=z32P[par][:, 512 * j:512 * (j + 1)], in_=ps_big[:, :]),
                   reads=[b_big], writes=[b_z32[par]])
            emit_ropefeat(c, zbfP[par][:, :], b_zbf[par])
            for src_bf, b_src, dst, b_dst in (
                    (zbfP[par][:, :], b_zbf[par], qfm, bq),
                    (k_tm[:, D * c:D * (c + 1)], b_ktm[c], kfm, bk)):
                for rnd in range(2):
                    for hh in range(8):
                        h = rnd * 8 + hh
                        OP("tensor", lambda e, h=h, hh=hh, s=src_bf, ps_tp2=ps_tp2: e.transpose(
                            ps_tp2[0:64, 128 * hh:128 * (hh + 1)],
                            s[:, 64 * h:64 * (h + 1)], identb[:, :]),
                           reads=[b_src, b_const], writes=[btp])
                    OP("vector", lambda e, rnd=rnd, d=dst, ps_tp2=ps_tp2: e.tensor_copy(
                        out=d[0:64, 1024 * rnd:1024 * (rnd + 1)], in_=ps_tp2[0:64, :]),
                       reads=[btp], writes=[b_dst])
            for g in range(4):
                for jj in range(4):
                    h = 4 * g + jj
                    OP("tensor", lambda e, h=h, jj=jj, qfm=qfm, kfm=kfm: e.matmul(
                        ps_S[:, 128 * jj:128 * (jj + 1)],
                        kfm[0:64, 128 * h:128 * (h + 1)],
                        qfm[0:64, 128 * h:128 * (h + 1)],
                        start=True, stop=True),
                       reads=[bk, bq], writes=[b_psS])
                OP("vector", lambda e: e.tensor_tensor(
                    out=smask[:, :], in0=ps_S[:, :], in1=masku4[:, :], op=ALU.mult),
                   reads=[b_psS, b_const], writes=[b_sm])
                for jj in range(4):
                    OP("tensor", lambda e, jj=jj: e.matmul(
                        ps_dr[:, jj:jj + 1],
                        smask[:, 128 * jj:128 * (jj + 1)], onesb[:, :],
                        start=True, stop=True),
                       reads=[b_sm, b_const], writes=[b_pdr])
                for jj in range(4):
                    h = 4 * g + jj
                    OP("tensor", lambda e, h=h, jj=jj, qfm=qfm: e.matmul(
                        ps_dr[:, 64 + jj:64 + jj + 1],
                        qfm[:, 128 * h:128 * (h + 1)], ksbf[:, h:h + 1],
                        start=True, stop=True),
                       reads=[bq, b_ksbf], writes=[b_pdr])
                ps_O = ps_OP[g % 2]
                b_psO = b_psOP[g % 2]
                for jj in range(4):
                    h = 4 * g + jj
                    OP("tensor", lambda e, h=h, jj=jj, c=c, ps_O=ps_O: e.matmul(
                        ps_O[:, 128 * jj:128 * (jj + 1)],
                        v_tm[:, D * c + 64 * h: D * c + 64 * (h + 1)],
                        smask[:, 128 * jj:128 * (jj + 1)],
                        start=True, stop=False),
                       reads=[b_vtm[c], b_sm], writes=[b_psO])
                    OP("tensor", lambda e, h=h, jj=jj, qfm=qfm, ps_O=ps_O: e.matmul(
                        ps_O[:, 128 * jj:128 * (jj + 1)],
                        KVbf[:, 64 * h:64 * (h + 1)],
                        qfm[:, 128 * h:128 * (h + 1)],
                        start=False, stop=True),
                       reads=[b_kvbf, bq], writes=[b_psO])
                OP("vector", lambda e: e.tensor_copy(
                    out=denA[:, :], in_=ps_dr[:, 0:4]),
                   reads=[b_pdr], writes=[b_denr])
                OP("vector", lambda e: e.tensor_tensor(
                    out=denA[:, :], in0=denA[:, :], in1=ps_dr[:, 64:68], op=ALU.add),
                   reads=[b_pdr, b_denr], writes=[b_denr])
                OP("vector", lambda e: e.tensor_scalar(
                    out=denr[:, 0:4], in0=denA[:, :],
                    scalar1=1e-6, scalar2=None, op0=ALU.max),
                   reads=[b_denr], writes=[b_denr])
                OP("vector", lambda e: e.reciprocal(denr[:, 0:4], denr[:, 0:4]),
                   reads=[b_denr], writes=[b_denr])
                OP("tensor", lambda e: e.transpose(
                    ps_dr[0:16, 128:256], denr[:, :], identf[:, :]),
                   reads=[b_denr, b_const], writes=[b_pdr])
                OP("vector", lambda e: e.tensor_copy(out=rT[:, :], in_=ps_dr[0:4, 128:256]),
                   reads=[b_pdr], writes=[b_rT])
                for jj in range(4):
                    OP("tensor", lambda e, jj=jj: e.matmul(
                        ps_o1[0:64, 128 * jj:128 * (jj + 1)],
                        esel4[:, 64 * jj:64 * (jj + 1)], rT[:, :],
                        start=True, stop=True),
                       reads=[b_rT, b_const], writes=[b_po1])
                OP("vector", lambda e: e.tensor_copy(out=divs[:, :], in_=ps_o1[0:64, :]),
                   reads=[b_po1], writes=[b_div])
                OP("vector", lambda e, g=g, ps_O=ps_O: e.tensor_tensor(
                    out=attn_c[0:64, 512 * g:512 * (g + 1)], in0=ps_O[:, :],
                    in1=divs[:, :], op=ALU.mult),
                   reads=[b_psO, b_div], writes=[b_attn])
            for rr in range(2):
                for hh in range(8):
                    h = rr * 8 + hh
                    OP("tensor", lambda e, h=h, hh=hh, c=c: e.matmul(
                        ps_big[0:64, 64 * hh:64 * (hh + 1)],
                        k_tm[:, D * c + 64 * h: D * c + 64 * (h + 1)],
                        v_tm[:, D * c + 64 * h: D * c + 64 * (h + 1)],
                        start=(hh == 0), stop=(hh == 7)),
                       reads=[b_ktm[c], b_vtm[c]], writes=[b_big])
                OP("vector", lambda e, rr=rr: e.tensor_tensor(
                    out=KV32[:, 512 * rr:512 * (rr + 1)],
                    in0=KV32[:, 512 * rr:512 * (rr + 1)],
                    in1=ps_big[0:64, :], op=ALU.add),
                   reads=[b_big, b_kv32], writes=[b_kv32])
            for h in range(H):
                OP("tensor", lambda e, h=h, c=c: e.matmul(
                    ps_dr[0:64, 32 + h:33 + h],
                    k_tm[:, D * c + 64 * h: D * c + 64 * (h + 1)], onesb[:, :],
                    start=(h == 0), stop=(h == 15)),
                   reads=[b_ktm[c], b_const], writes=[b_pdr])
            OP("vector", lambda e: e.tensor_tensor(
                out=ks32[:, :], in0=ks32[:, :], in1=ps_dr[0:64, 32:48], op=ALU.add),
               reads=[b_pdr, b_ks32], writes=[b_ks32])
            OP("scalar", lambda e: e.copy(out=KVbf[0:64, :], in_=KV32[:, :]),
               reads=[b_kv32], writes=[b_kvbf])
            OP("scalar", lambda e: e.copy(out=ksbf[0:64, :], in_=ks32[:, :]),
               reads=[b_ks32], writes=[b_ksbf])
            for rnd in range(2):
                for dsl in range(4):
                    ds = rnd * 4 + dsl
                    for h in range(H):
                        OP("tensor", lambda e, ds=ds, dsl=dsl, h=h: e.matmul(
                            ps_o1[:, 128 * dsl:128 * (dsl + 1)],
                            wo_bf[:, 1024 * h + 128 * ds: 1024 * h + 128 * (ds + 1)],
                            attn_c[:, 128 * h:128 * (h + 1)],
                            start=(h == 0), stop=(h == H - 1)),
                           reads=[b_wo, b_attn], writes=[b_po1])
                OP("vector", lambda e, rnd=rnd: e.tensor_copy(
                    out=o1_bf[:, 512 * rnd:512 * (rnd + 1)], in_=ps_o1[:, :]),
                   reads=[b_po1], writes=[b_o1])
            for ds in range(8):
                OP("tensor", lambda e, ds=ds, ps_tp2=ps_tp2: e.transpose(
                    ps_tp2[:, 128 * ds:128 * (ds + 1)],
                    o1_bf[:, 128 * ds:128 * (ds + 1)], identb[:, :]),
                   reads=[b_o1, b_const], writes=[btp])
            OP("vector", lambda e, c=c, ps_tp2=ps_tp2: e.tensor_tensor(
                out=x_bf[:, D * c:D * (c + 1)], in0=x_bf[:, D * c:D * (c + 1)],
                in1=ps_tp2[:, :], op=ALU.add),
               reads=[btp], writes=[b_x[c]])
            if debug and c == 0:
                dma("gpsimd", dbg["d_attn"], attn_c[0:64, :], reads=[b_attn])
                dma("gpsimd", dbg["d_x1"], x_bf[:, 0:D], reads=[b_x[0]])
        pb.barrier()

    p12.close()

    with ExitStack() as p3:
        sb3 = lambda name, shape, dt: p3.enter_context(nc.sbuf_tensor(name, shape, dt))
        ps3 = lambda name, shape, dt: p3.enter_context(nc.psum_tensor(name, shape, dt))
        wr_bf = sb3("wr_bf", [128, 8 * 8], BF16)
        wT_sbP = [sb3("wT_sb%d" % p, [8, 1024], BF16) for p in range(2)]
        wg_sb = sb3("wg_sb", [128, 8192], BF16)
        wu_sb = sb3("wu_sb", [128, 8192], BF16)
        wd_sb = sb3("wd_sb", [128, 8192], BF16)
        xn2fmP = [sb3("xn2fm%d" % p, [128, 8192], BF16) for p in range(2)]
        h_all = sb3("h_all", [128, 8192], BF16)
        acc = sb3("acc", [128, 8192], F32)
        wb_sb = sb3("wb_sb", [128, 1024], BF16)
        sg_sbP = [sb3("sg_sb%d" % p, [128, 512], BF16) for p in range(2)]
        hp_sb1 = sb3("hp_sb1", [128, 512], BF16)
        hp_sbP = [hp_sb1, hp_sb1]
        out_st = sb3("out_st", [128, 512], F32)
        rt = {nm: sb3("rt_" + nm, [128, 32], F32) for nm in
              ("e", "inh", "z1", "z2", "zone", "dist", "w", "fb", "m1x", "fb2",
               "fbw", "wm", "wfin")}
        rs = {nm: sb3("rs_" + nm, [128, 4], F32) for nm in
              ("wsum", "nex", "nex1", "m1", "m2", "fbs", "rec", "ws2", "rec2")}
        ntr_sb = sb3("ntr_sb", [128, 1], F32)
        rs["ntr"] = ntr_sb
        b_ntr = Buf("ntr")

        ps_tp3 = ps3("ps_tp3", [128, 1024], BF16)
        ps_guP = [ps3("ps_gu%d" % p, [128, 1024], F32) for p in range(2)]
        ps_dn = ps3("ps_dn", [128, 1024], F32)
        ps_ft = ps3("ps_ft", [128, 512], F32)

        b_wr = Buf("wr")
        b_wT = [Buf("wT%d" % p) for p in range(2)]
        b_wg = Buf("wg")
        b_wu = Buf("wu")
        b_wd = Buf("wd")
        b_xn2 = [Buf("xn2_%d" % p) for p in range(2)]
        b_h = Buf("h_all")
        b_acc = Buf("acc")
        b_wb = Buf("wb")
        b_sgP = [Buf("sg0"), Buf("sg1")]
        b_hp1 = Buf("hp1")
        b_hpP = [b_hp1, b_hp1]
        b_ost = Buf("ost")
        b_rt = Buf("rt")
        b_tp3 = Buf("tp3")
        b_gu = [Buf("gu%d" % p) for p in range(2)]
        b_pdnH = [Buf("pdn0"), Buf("pdn1")]
        b_pft = Buf("pft")

        dma("gpsimd", wr_bf[:, :].rearrange("p (dc n) -> p dc n", n=8),
            wr_h.rearrange("(dc p) n -> p dc n", p=128), writes=[b_wr])
        OP("vector", lambda e: e.reciprocal(rs["ntr"][:, :], temp_sb[:, :]),
           reads=[b_const], writes=[b_ntr])
        OP("vector", lambda e: e.tensor_scalar(
            out=rs["ntr"][:, :], in0=rs["ntr"][:, :], scalar1=-1.0, scalar2=None,
            op0=ALU.mult), reads=[b_ntr], writes=[b_ntr])

        def emit_routing_batch(bi):
            cs0 = bi * 4
            hf = cs0 // 8
            xn2fm = xn2fmP[hf]
            for ci in range(4):
                c = cs0 + ci
                tc = c % 8
                L = emit_ln(c, ps_tp3, b_tp3)
                OP("vector", lambda e, tc=tc, xn2fm=xn2fm: e.tensor_copy(
                    out=xn2fm[:, :].rearrange("p (dc n) -> p dc n", n=1024)[:, :, 128 * tc:128 * (tc + 1)],
                    in_=ps_tp3[:, :].rearrange("p (dc n) -> p dc n", n=128)),
                   reads=[b_tp3], writes=[b_xn2[hf]])
                for dc in range(8):
                    OP("tensor", lambda e, dc=dc, ci=ci, L=L: e.matmul(
                        ps_ft[:, 8 * ci:8 * (ci + 1)],
                        L["xnfm"][:, 128 * dc:128 * (dc + 1)],
                        wr_bf[:, 8 * dc:8 * (dc + 1)],
                        start=(dc == 0), stop=(dc == 7)),
                       reads=[L["b_xnfm"], b_wr], writes=[b_pft])
            V, S = "vector", "scalar"

            def t3(t):
                return t[:, :].rearrange("p (c e) -> p c e", e=8)

            def s3(t):
                return t[:, :].rearrange("p (c e) -> p c e", e=1)

            def btt(e, out_ap, a_ap, b_ap, op):
                aa, bb = bass.broadcast_tensor_aps(a_ap, b_ap)
                return e.tensor_tensor(out=out_ap, in0=aa, in1=bb, op=op)

            OP(S, lambda e: e.activation(out=rt["e"][:, :], in_=ps_ft[:, 0:32],
                                         func=AF.Exp, scale=rs["ntr"][:, :]),
               reads=[b_pft, b_ntr], writes=[b_rt])
            OP(V, lambda e: e.tensor_scalar(out=rt["inh"][:, :], in0=rt["e"][:, :],
                                            scalar1=1.0, scalar2=None, op0=ALU.add),
               reads=[b_rt], writes=[b_rt])
            OP(V, lambda e: e.reciprocal(rt["inh"][:, :], rt["inh"][:, :]),
               reads=[b_rt], writes=[b_rt])
            OP(V, lambda e: e.tensor_single_scalar(out=rt["z1"][:, :], in_=rt["inh"][:, :],
                                                   scalar=GL, op=ALU.is_ge),
               reads=[b_rt], writes=[b_rt])
            OP(V, lambda e: e.tensor_single_scalar(out=rt["z2"][:, :], in_=rt["inh"][:, :],
                                                   scalar=GU, op=ALU.is_le),
               reads=[b_rt], writes=[b_rt])
            OP(V, lambda e: e.tensor_tensor(out=rt["zone"][:, :], in0=rt["z1"][:, :],
                                            in1=rt["z2"][:, :], op=ALU.mult),
               reads=[b_rt], writes=[b_rt])
            OP(V, lambda e: e.tensor_scalar(out=rt["dist"][:, :], in0=rt["inh"][:, :],
                                            scalar1=GC, scalar2=None, op0=ALU.subtract),
               reads=[b_rt], writes=[b_rt])
            OP(S, lambda e: e.activation(out=rt["dist"][:, :], in_=rt["dist"][:, :],
                                         func=AF.Abs),
               reads=[b_rt], writes=[b_rt])
            OP(S, lambda e: e.activation(out=rt["w"][:, :], in_=rt["dist"][:, :],
                                         func=AF.Exp, scale=C_N10()),
               reads=[b_rt, b_const], writes=[b_rt])
            OP(V, lambda e: e.tensor_tensor(out=rt["w"][:, :], in0=rt["w"][:, :],
                                            in1=rt["zone"][:, :], op=ALU.mult),
               reads=[b_rt], writes=[b_rt])
            OP(V, lambda e: e.reduce_sum(s3(rs["wsum"]), t3(rt["w"]), axis=AX),
               reads=[b_rt], writes=[b_rt])
            OP(V, lambda e: e.tensor_single_scalar(out=rs["nex"][:, :], in_=rs["wsum"][:, :],
                                                   scalar=1e-8, op=ALU.is_lt),
               reads=[b_rt], writes=[b_rt])
            OP(S, lambda e: e.activation(out=rt["fb"][:, :], in_=rt["dist"][:, :],
                                         func=AF.Exp, scale=C_N103()),
               reads=[b_rt, b_const], writes=[b_rt])
            OP(V, lambda e: e.reduce_max(s3(rs["m1"]), t3(rt["fb"]), axis=AX),
               reads=[b_rt], writes=[b_rt])
            OP(V, lambda e: btt(e, t3(rt["m1x"]), t3(rt["fb"]), s3(rs["m1"]), ALU.is_ge),
               reads=[b_rt], writes=[b_rt])
            OP(V, lambda e: e.tensor_scalar(out=rt["fb2"][:, :], in0=rt["fb"][:, :],
                                            scalar1=1.0, scalar2=None, op0=ALU.add),
               reads=[b_rt], writes=[b_rt])
            OP(V, lambda e: e.tensor_tensor(out=rt["fb2"][:, :], in0=rt["fb2"][:, :],
                                            in1=rt["m1x"][:, :], op=ALU.mult),
               reads=[b_rt], writes=[b_rt])
            OP(V, lambda e: e.tensor_tensor(out=rt["fb2"][:, :], in0=rt["fb"][:, :],
                                            in1=rt["fb2"][:, :], op=ALU.subtract),
               reads=[b_rt], writes=[b_rt])
            OP(V, lambda e: e.reduce_max(s3(rs["m2"]), t3(rt["fb2"]), axis=AX),
               reads=[b_rt], writes=[b_rt])
            OP(V, lambda e: btt(e, t3(rt["m1x"]), t3(rt["fb"]), s3(rs["m2"]), ALU.is_ge),
               reads=[b_rt], writes=[b_rt])
            OP(V, lambda e: e.tensor_tensor(out=rt["fbw"][:, :], in0=rt["fb"][:, :],
                                            in1=rt["m1x"][:, :], op=ALU.mult),
               reads=[b_rt], writes=[b_rt])
            OP(V, lambda e: e.reduce_sum(s3(rs["fbs"]), t3(rt["fbw"]), axis=AX),
               reads=[b_rt], writes=[b_rt])
            OP(V, lambda e: e.tensor_single_scalar(out=rs["fbs"][:, :], in_=rs["fbs"][:, :],
                                                   scalar=1e-8, op=ALU.max),
               reads=[b_rt], writes=[b_rt])
            OP(V, lambda e: e.reciprocal(rs["rec"][:, :], rs["fbs"][:, :]),
               reads=[b_rt], writes=[b_rt])
            OP(V, lambda e: btt(e, t3(rt["fbw"]), t3(rt["fbw"]), s3(rs["rec"]), ALU.mult),
               reads=[b_rt], writes=[b_rt])
            OP(V, lambda e: e.tensor_scalar(out=rs["nex1"][:, :], in0=rs["nex"][:, :],
                                            scalar1=-1.0, scalar2=1.0, op0=ALU.mult, op1=ALU.add),
               reads=[b_rt], writes=[b_rt])
            OP(V, lambda e: btt(e, t3(rt["w"]), t3(rt["w"]), s3(rs["nex1"]), ALU.mult),
               reads=[b_rt], writes=[b_rt])
            OP(V, lambda e: btt(e, t3(rt["fbw"]), t3(rt["fbw"]), s3(rs["nex"]), ALU.mult),
               reads=[b_rt], writes=[b_rt])
            OP(V, lambda e: e.tensor_tensor(out=rt["wm"][:, :], in0=rt["w"][:, :],
                                            in1=rt["fbw"][:, :], op=ALU.add),
               reads=[b_rt], writes=[b_rt])
            OP(V, lambda e: e.reduce_sum(s3(rs["ws2"]), t3(rt["wm"]), axis=AX),
               reads=[b_rt], writes=[b_rt])
            OP(V, lambda e: e.tensor_single_scalar(out=rs["ws2"][:, :], in_=rs["ws2"][:, :],
                                                   scalar=1e-8, op=ALU.max),
               reads=[b_rt], writes=[b_rt])
            OP(V, lambda e: e.reciprocal(rs["rec2"][:, :], rs["ws2"][:, :]),
               reads=[b_rt], writes=[b_rt])
            OP(V, lambda e: btt(e, t3(rt["wfin"]), t3(rt["wm"]), s3(rs["rec2"]), ALU.mult),
               reads=[b_rt], writes=[b_rt])
            for ci in range(4):
                c = cs0 + ci
                tc = c % 8
                OP("tensor", lambda e, ci=ci: e.transpose(
                    ps_ft[0:8, 128:256], rt["wfin"][:, 8 * ci:8 * (ci + 1)],
                    identf[:, :]),
                   reads=[b_rt, b_const], writes=[b_pft])
                OP("vector", lambda e, tc=tc, hf=hf: e.tensor_copy(
                    out=wT_sbP[hf][:, 128 * tc:128 * (tc + 1)], in_=ps_ft[0:8, 128:256]),
                   reads=[b_pft], writes=[b_wT[hf]])

        for hf in range(2):
            for bi in (2 * hf, 2 * hf + 1):
                emit_routing_batch(bi)
            xn2fm = xn2fmP[hf]
            for ei in range(E):
                dma("gpsimd", wg_sb[:, :].rearrange("p (dc n) -> p dc n", n=1024),
                    wg_h[ei].rearrange("(dc p) n -> p dc n", p=128), writes=[b_wg])
                dma("gpsimd", wu_sb[:, :].rearrange("p (dc n) -> p dc n", n=1024),
                    wu_h[ei].rearrange("(dc p) n -> p dc n", p=128), writes=[b_wu])
                dma("gpsimd", wd_sb[:, :].rearrange("p (ic n) -> p ic n", n=1024),
                    wd_h[ei].rearrange("(ic p) n -> p ic n", p=128), writes=[b_wd])
                for j in range(2):
                    OP("tensor", lambda e, j=j, ei=ei, hf=hf: e.matmul(
                        ps_ft[:, :],
                        e8sel[:, 128 * ei:128 * (ei + 1)],
                        wT_sbP[hf][:, 512 * j:512 * (j + 1)],
                        start=True, stop=True),
                       reads=[b_wT[hf], b_const], writes=[b_pft])
                    OP("vector", lambda e, j=j: e.tensor_copy(
                        out=wb_sb[:, 512 * j:512 * (j + 1)], in_=ps_ft[:, :]),
                       reads=[b_pft], writes=[b_wb])
                for ic in range(8):
                    for tt in range(2):
                        ps_gu = ps_guP[tt]
                        bgu = b_gu[tt]
                        for dc in range(8):
                            OP("tensor", lambda e, ic=ic, tt=tt, dc=dc, ps_gu=ps_gu, xn2fm=xn2fm: e.matmul(
                                ps_gu[:, 0:512],
                                wg_sb[:, 1024 * dc + 128 * ic: 1024 * dc + 128 * (ic + 1)],
                                xn2fm[:, 1024 * dc + 512 * tt: 1024 * dc + 512 * (tt + 1)],
                                start=(dc == 0), stop=(dc == 7)),
                               reads=[b_wg, b_xn2[hf]], writes=[bgu])
                        for dc in range(8):
                            OP("tensor", lambda e, ic=ic, tt=tt, dc=dc, ps_gu=ps_gu, xn2fm=xn2fm: e.matmul(
                                ps_gu[:, 512:1024],
                                wu_sb[:, 1024 * dc + 128 * ic: 1024 * dc + 128 * (ic + 1)],
                                xn2fm[:, 1024 * dc + 512 * tt: 1024 * dc + 512 * (tt + 1)],
                                start=(dc == 0), stop=(dc == 7)),
                               reads=[b_wu, b_xn2[hf]], writes=[bgu])
                        OP("scalar", lambda e, ps_gu=ps_gu, tt=tt: e.activation(
                            out=sg_sbP[tt][:, :], in_=ps_gu[:, 0:512], func=AF.Silu),
                           reads=[bgu], writes=[b_sgP[tt]])
                        OP("vector", lambda e, ps_gu=ps_gu, tt=tt: e.tensor_tensor(
                            out=hp_sbP[tt][:, :], in0=sg_sbP[tt][:, :], in1=ps_gu[:, 512:1024],
                            op=ALU.mult),
                           reads=[b_sgP[tt], bgu], writes=[b_hpP[tt]])
                        OP("vector", lambda e, ic=ic, tt=tt: e.tensor_tensor(
                            out=h_all[:, 1024 * ic + 512 * tt: 1024 * ic + 512 * (tt + 1)],
                            in0=hp_sbP[tt][:, :], in1=wb_sb[:, 512 * tt:512 * (tt + 1)],
                            op=ALU.mult),
                           reads=[b_hpP[tt], b_wb], writes=[b_h])
                for tt in range(2):
                    for dc in range(8):
                        hd = dc % 2
                        for ic in range(8):
                            OP("tensor", lambda e, tt=tt, dc=dc, ic=ic, hd=hd: e.matmul(
                                ps_dn[:, 512 * hd:512 * (hd + 1)],
                                wd_sb[:, 1024 * ic + 128 * dc: 1024 * ic + 128 * (dc + 1)],
                                h_all[:, 1024 * ic + 512 * tt: 1024 * ic + 512 * (tt + 1)],
                                start=(ic == 0), stop=(ic == 7)),
                               reads=[b_wd, b_h], writes=[b_pdnH[hd]])
                        if ei == 0:
                            OP("vector", lambda e, tt=tt, dc=dc, hd=hd: e.tensor_copy(
                                out=acc[:, 1024 * dc + 512 * tt: 1024 * dc + 512 * (tt + 1)],
                                in_=ps_dn[:, 512 * hd:512 * (hd + 1)]),
                               reads=[b_pdnH[hd]], writes=[b_acc])
                        else:
                            OP("vector", lambda e, tt=tt, dc=dc, hd=hd: e.tensor_tensor(
                                out=acc[:, 1024 * dc + 512 * tt: 1024 * dc + 512 * (tt + 1)],
                                in0=acc[:, 1024 * dc + 512 * tt: 1024 * dc + 512 * (tt + 1)],
                                in1=ps_dn[:, 512 * hd:512 * (hd + 1)], op=ALU.add),
                               reads=[b_pdnH[hd], b_acc], writes=[b_acc])
            for tc in range(8):
                c = hf * 8 + tc
                for rnd in range(2):
                    for dcl in range(4):
                        dc = rnd * 4 + dcl
                        OP("tensor", lambda e, tc=tc, dc=dc, dcl=dcl: e.transpose(
                            ps_ft[:, 128 * dcl:128 * (dcl + 1)],
                            acc[:, 1024 * dc + 128 * tc: 1024 * dc + 128 * (tc + 1)],
                            identf[:, :]),
                           reads=[b_acc, b_const], writes=[b_pft])
                    OP("vector", lambda e, c=c, rnd=rnd: e.tensor_tensor(
                        out=out_st[:, :],
                        in0=x_bf[:, D * c + 512 * rnd: D * c + 512 * (rnd + 1)],
                        in1=ps_ft[:, :], op=ALU.add),
                       reads=[b_pft, b_x[c]], writes=[b_ost])
                    out_toks.append(dma(
                        "sync", out_h[128 * c:128 * (c + 1), 512 * rnd:512 * (rnd + 1)],
                        out_st[:, :], reads=[b_ost]))

    OP("gpsimd", lambda e: e.engine_nop(), deps=list(out_toks))
    pb.finalize_and_emit(nc, ctx)
    return nc


_NC_CACHE = {}


def _get_nc(debug=False):
    key = bool(debug)
    if key not in _NC_CACHE:
        _NC_CACHE[key] = build_program(debug=key)
    return _NC_CACHE[key]


def make_inmaps(x_full, wqkv, wout, wg, wu, wd, wr, temp):
    x_full = np.ascontiguousarray(x_full.reshape(NCORES * M, D).astype(np.float32))
    half = 32
    inv = 1.0 / (10000.0 ** (np.arange(half, dtype=np.float64) / half))
    masku = (np.arange(128)[:, None] <= np.arange(128)[None, :]).astype(np.float32)
    masku4 = np.tile(masku, (1, 4)).astype(ml_dtypes.bfloat16)
    esel4 = np.zeros((4, 256), ml_dtypes.bfloat16)
    for j in range(4):
        esel4[j, 64 * j:64 * (j + 1)] = 1.0
    e8sel = np.zeros((8, 1024), ml_dtypes.bfloat16)
    for j in range(8):
        e8sel[j, 128 * j:128 * (j + 1)] = 1.0
    identb = np.eye(128).astype(ml_dtypes.bfloat16)
    identf = np.eye(128, dtype=np.float32)
    onesb = np.ones((128, 1), ml_dtypes.bfloat16)
    temp1 = np.full((128, 1), float(temp), np.float32)
    consts = np.zeros((128, 8), np.float32)
    consts[:, 0] = 1.0 / D
    consts[:, 1] = 1e-5
    consts[:, 2] = SCALE
    consts[:, 3] = -10.0
    consts[:, 4] = -10.0 / 3.0
    in_maps = []
    for i in range(NCORES):
        pos = np.arange(i * M, (i + 1) * M, dtype=np.float64)[:, None] * inv[None, :]
        cosr = np.cos(pos).astype(np.float32).astype(ml_dtypes.bfloat16)
        sinr = np.sin(pos).astype(np.float32).astype(ml_dtypes.bfloat16)
        prefmask = np.zeros((64, 8), np.float32)
        prefmask[:, :i] = 1.0
        in_maps.append({
            "x": x_full[i * M:(i + 1) * M],
            "wqkv": wqkv, "wout": wout, "wg": wg, "wu": wu, "wd": wd, "wr": wr,
            "temp1": temp1, "cosr": cosr, "sinr": sinr,
            "identb": identb, "identf": identf, "masku4": masku4,
            "onesb": onesb, "esel4": esel4, "e8sel": e8sel,
            "prefmask": prefmask, "consts": consts,
        })
    return in_maps


def kernel(L0_attn_x, L0_attn_w_qkv, L0_attn_w_out, L0_ffn_Wg, L0_ffn_Wu,
           L0_ffn_Wd, L0_ffn_Wr, L0_ffn_temperature):
    x = np.asarray(L0_attn_x)
    in_maps = make_inmaps(x, np.asarray(L0_attn_w_qkv, np.float32),
                          np.asarray(L0_attn_w_out, np.float32),
                          np.asarray(L0_ffn_Wg, np.float32),
                          np.asarray(L0_ffn_Wu, np.float32),
                          np.asarray(L0_ffn_Wd, np.float32),
                          np.asarray(L0_ffn_Wr, np.float32),
                          float(np.asarray(L0_ffn_temperature).reshape(-1)[0]))
    nc = _get_nc(debug=False)
    res = run_bass_kernel_spmd(nc, in_maps, core_ids=list(range(NCORES)))
    out = np.concatenate([r["out"] for r in res.results], axis=0)
    return out.reshape(1, NCORES * M, D).astype(np.float32)


# version 12
# speedup vs baseline: 1.0757x; 1.0757x over previous
import numpy as np
import ml_dtypes
import concourse.bass as bass
import concourse.mybir as mybir
from concourse.bass_utils import run_bass_kernel_spmd
from contextlib import ExitStack

F32 = mybir.dt.float32
BF16 = mybir.dt.bfloat16
AF = mybir.ActivationFunctionType
ALU = mybir.AluOpType
AX = mybir.AxisListType.X

NCORES = 8
M = 2048
D = 1024
H = 16
DH = 64
CH = 128
NCH = M // CH
SCALE = float(DH ** (-0.25))
E = 8
GC = 1.0 / float(np.e)
GL = 0.5 - float(np.log(4.0 / 3.0))
GU = 0.5

ENGINES = ("sync", "gpsimd", "vector", "scalar", "tensor")


class Buf:
    __slots__ = ("writer", "readers", "name")

    def __init__(self, name=""):
        self.writer = None
        self.readers = []
        self.name = name


class PB:
    def __init__(self):
        self.ops = {e: [] for e in ENGINES}
        self.floor = []
        self.gseq = 0

    def op(self, eng, fn, reads=(), writes=(), deps=(), fence=False, dma=False,
           is_async=False, grp=None):
        if eng == "vector":
            fence = True
        dd = list(deps)
        for b in reads:
            if b.writer is not None:
                dd.append(b.writer)
        for b in writes:
            if b.writer is not None:
                dd.append(b.writer)
            dd.extend(b.readers)
        dd.extend(self.floor)
        idx = len(self.ops[eng])
        self.gseq += 1
        self.ops[eng].append({"fn": fn, "deps": dd, "dma": dma, "fence": fence,
                              "async": is_async or dma, "mark": False,
                              "wkey": grp or tuple(id(b) for b in writes),
                              "gseq": self.gseq})
        tok = (eng, idx)
        for b in reads:
            b.readers.append(tok)
        for b in writes:
            b.readers = []
            b.writer = tok
        return tok

    def barrier(self):
        self.floor = [(e, len(self.ops[e]) - 1) for e in ENGINES if self.ops[e]]

    def schedule(self):
        EST = {"sync": 800, "gpsimd": 1500, "vector": 500, "scalar": 600,
               "tensor": 220}
        XLAT = 250
        nodes = []
        idmap = {}
        for e in ENGINES:
            i = 0
            n = len(self.ops[e])
            while i < n:
                run = [i]
                if e == "tensor" and self.ops[e][i]["wkey"]:
                    k = self.ops[e][i]["wkey"]
                    j = i + 1
                    while (j < n and self.ops[e][j]["wkey"] == k
                           and self.ops[e][j]["gseq"] == self.ops[e][j - 1]["gseq"] + 1):
                        run.append(j)
                        j += 1
                ni = len(nodes)
                nodes.append((e, run))
                for idx0 in run:
                    idmap[(e, idx0)] = ni
                i = run[-1] + 1
        ndeps = [[] for _ in nodes]
        succs = [[] for _ in nodes]
        indeg = [0] * len(nodes)
        for e in ENGINES:
            for i, o in enumerate(self.ops[e]):
                ni = idmap[(e, i)]
                for d in o["deps"]:
                    nd = idmap[d]
                    if nd == ni:
                        continue
                    ndeps[ni].append(nd)
        for ni in range(len(nodes)):
            ndeps[ni] = sorted(set(ndeps[ni]))
            for nd in ndeps[ni]:
                succs[nd].append(ni)
            indeg[ni] = len(ndeps[ni])
        prio = [0] * len(nodes)
        dur_n = [EST[nodes[ni][0]] * len(nodes[ni][1]) for ni in range(len(nodes))]
        topo = []
        tmp_in = indeg[:]
        import collections
        dq = collections.deque(ni for ni in range(len(nodes)) if tmp_in[ni] == 0)
        while dq:
            ni = dq.popleft()
            topo.append(ni)
            for s in succs[ni]:
                tmp_in[s] -= 1
                if tmp_in[s] == 0:
                    dq.append(s)
        assert len(topo) == len(nodes), "cycle in dep graph"
        for ni in reversed(topo):
            e = nodes[ni][0]
            best = 0
            for s in succs[ni]:
                lat = XLAT if nodes[s][0] != e else 0
                if prio[s] + lat > best:
                    best = prio[s] + lat
            prio[ni] = best + dur_n[ni]
        import os
        keep = set((os.environ.get("KKEEP") or "").split(","))
        import heapq
        ready = {e: [] for e in ENGINES}
        unmet = indeg[:]
        finish = [0.0] * len(nodes)
        for ni in range(len(nodes)):
            if unmet[ni] == 0:
                e0 = nodes[ni][0]
                key = (ni if e0 in keep else -prio[ni])
                heapq.heappush(ready[e0], (key, ni))
        order = {e: [] for e in ENGINES}
        etime = {e: 0.0 for e in ENGINES}
        import itertools
        n_done = 0
        while n_done < len(nodes):
            cand = [e for e in ENGINES if ready[e]]
            if not cand:
                raise RuntimeError("scheduler deadlock")
            e = min(cand, key=lambda x: etime[x])
            _, ni = heapq.heappop(ready[e])
            t0 = etime[e]
            for nd in ndeps[ni]:
                de = nodes[nd][0]
                lat = XLAT if de != e else 0
                if finish[nd] + lat > t0:
                    t0 = finish[nd] + lat
            t1 = t0 + dur_n[ni]
            etime[e] = t1
            finish[ni] = t1
            order[e].extend(nodes[ni][1])
            n_done += 1
            for s in succs[ni]:
                unmet[s] -= 1
                if unmet[s] == 0:
                    heapq.heappush(ready[nodes[s][0]], (-prio[s], s))
        import os
        ksch = set((os.environ.get("KSCHED") or "sync,gpsimd,vector,scalar,tensor").split(","))
        for e in ENGINES:
            if e not in ksch:
                order[e] = list(range(len(self.ops[e])))
        remap = {}
        newops = {}
        for e in ENGINES:
            newops[e] = [self.ops[e][i] for i in order[e]]
            for newi, oldi in enumerate(order[e]):
                remap[(e, oldi)] = (e, newi)
        for e in ENGINES:
            for o in newops[e]:
                o["deps"] = [remap[d] for d in o["deps"]]
        self.ops = newops

    def finalize_and_emit(self, nc, ctx):
        import os
        if not os.environ.get("KNOSCHED"):
            self.schedule()
        for e in ENGINES:
            for i, o in enumerate(self.ops[e]):
                seen = {}
                for (de, di) in o["deps"]:
                    if de == e and di >= i:
                        continue
                    if (de == e and not o["fence"]
                            and not self.ops[de][di]["async"]):
                        continue
                    if seen.get(de, -1) < di:
                        seen[de] = di
                o["waits"] = seen
                for de, di in seen.items():
                    self.ops[de][di]["mark"] = True
        for e in ENGINES:
            for o in self.ops[e]:
                if o["dma"]:
                    o["mark"] = True
        semval = {e: [] for e in ENGINES}
        for e in ENGINES:
            v = 0
            for o in self.ops[e]:
                if o["mark"]:
                    v += 16 if o["dma"] else 1
                semval[e].append(v)
        sems = {e: ctx.enter_context(nc.semaphore("sem_" + e)) for e in ENGINES}
        block = ctx.enter_context(nc.Block())

        def make_section(ename):
            def section(eng):
                water = {e: 0 for e in ENGINES}
                for o in self.ops[ename]:
                    for de, di in sorted(o["waits"].items()):
                        v = semval[de][di]
                        if v > water[de]:
                            eng.wait_ge(sems[de], v)
                            water[de] = v
                    inst = o["fn"](eng)
                    if o["mark"]:
                        inst.then_inc(sems[ename], 16 if o["dma"] else 1)
            return section

        block.sync(make_section("sync"))
        block.gpsimd(make_section("gpsimd"))
        block.vector(make_section("vector"))
        block.scalar(make_section("scalar"))
        block.tensor(make_section("tensor"))


def build_program(debug=False):
    nc = bass.Bass("TRN2", target_bir_lowering=False, debug=False)

    x_in = nc.dram_tensor("x", [M, D], F32, kind="ExternalInput").ap()
    wqkv = nc.dram_tensor("wqkv", [D, 3 * D], F32, kind="ExternalInput").ap()
    wout = nc.dram_tensor("wout", [D, D], F32, kind="ExternalInput").ap()
    wg_h = nc.dram_tensor("wg", [E, D, D], F32, kind="ExternalInput").ap()
    wu_h = nc.dram_tensor("wu", [E, D, D], F32, kind="ExternalInput").ap()
    wd_h = nc.dram_tensor("wd", [E, D, D], F32, kind="ExternalInput").ap()
    wr_h = nc.dram_tensor("wr", [D, E], F32, kind="ExternalInput").ap()
    temp_h = nc.dram_tensor("temp1", [128, 1], F32, kind="ExternalInput").ap()
    cos_h = nc.dram_tensor("cosr", [M, 32], BF16, kind="ExternalInput").ap()
    sin_h = nc.dram_tensor("sinr", [M, 32], BF16, kind="ExternalInput").ap()
    identb_h = nc.dram_tensor("identb", [128, 128], BF16, kind="ExternalInput").ap()
    identf_h = nc.dram_tensor("identf", [128, 128], F32, kind="ExternalInput").ap()
    masku4_h = nc.dram_tensor("masku4", [128, 512], BF16, kind="ExternalInput").ap()
    onesb_h = nc.dram_tensor("onesb", [128, 1], BF16, kind="ExternalInput").ap()
    esel4_h = nc.dram_tensor("esel4", [4, 256], BF16, kind="ExternalInput").ap()
    e8sel_h = nc.dram_tensor("e8sel", [8, 1024], BF16, kind="ExternalInput").ap()
    prefm_h = nc.dram_tensor("prefmask", [64, 8], F32, kind="ExternalInput").ap()
    consts_h = nc.dram_tensor("consts", [128, 8], F32, kind="ExternalInput").ap()
    out_h = nc.dram_tensor("out", [M, D], F32, kind="ExternalOutput").ap()

    dbg = {}
    if debug:
        for nm, shape in [("d_attn", [64, 2048]), ("d_x1", [128, 1024]),
                          ("d_wfin", [128, 8]), ("d_kv", [64, 1024]),
                          ("d_ks", [64, 16])]:
            dbg[nm] = nc.dram_tensor(nm, shape, F32, kind="ExternalOutput").ap()

    kv_loc = nc.dram_tensor("kv_loc", [64, 1040], BF16, kind="Internal")
    kv_all = nc.dram_tensor("kv_all", [NCORES * 64, 1040], BF16, kind="Internal")

    pb = PB()
    ctx = ExitStack()
    out_toks = []
    OP = pb.op

    sb = lambda name, shape, dt: ctx.enter_context(nc.sbuf_tensor(name, shape, dt))

    x_bf = sb("x_bf", [128, NCH * D], BF16)
    identb = sb("s_identb", [128, 128], BF16)
    identf = sb("s_identf", [128, 128], F32)
    masku4 = sb("s_masku4", [128, 512], BF16)
    cos_sb = sb("cos_sb", [128, NCH * 32], BF16)
    sin_sb = sb("sin_sb", [128, NCH * 32], BF16)
    onesb = sb("s_onesb", [128, 1], BF16)
    esel4 = sb("s_esel4", [4, 256], BF16)
    e8sel = sb("s_e8sel", [8, 1024], BF16)
    prefm = sb("prefm", [64, 8], F32)
    temp_sb = sb("temp_sb", [128, 1], F32)
    csts = sb("csts", [128, 8], F32)
    C_INVD = lambda: csts[:, 0:1]
    C_EPS = lambda: csts[:, 1:2]
    C_SCALE = lambda: csts[:, 2:3]
    C_N10 = lambda: csts[:, 3:4]
    C_N103 = lambda: csts[:, 4:5]
    KV32 = sb("KV32", [64, 1024], F32)
    ks32 = sb("ks32", [64, 16], F32)
    KVbf = sb("KVbf", [128, 1024], BF16)
    ksbf = sb("ksbf", [128, 16], BF16)
    _rpa = sb("rpas", [128, D], F32)
    _lncen = sb("ln_cen", [128, D], F32)
    _lnm = sb("ln_m", [128, 1], F32)
    _lnss = sb("ln_ss", [128, 1], F32)
    _lnrs = sb("ln_rs", [128, 1], F32)
    _lnxnb = sb("xn_bfs", [128, D], BF16)
    _bln = Buf("ln")
    _bxnb = Buf("xnb")
    lnP = []
    for p in range(2):
        d = dict(cen=_lncen, sq=_rpa, m=_lnm, ss=_lnss, rs=_lnrs, xnb=_lnxnb,
                 xnfm=sb("xn_fm%d" % p, [128, D], BF16))
        d["b"] = _bln
        d["b_xnb"] = _bxnb
        d["b_xnfm"] = Buf("xnfm%d" % p)
        lnP.append(d)
    _z32 = sb("z32s", [128, D], F32)
    _rpb = sb("rpbs", [128, D], BF16)
    _zbf = sb("zbfs", [128, D], BF16)
    z32P = [_z32, _z32]
    rpaP = [_rpa, _rpa]
    rpbP = [_rpb, _rpb]
    zbfP = [_zbf, _zbf]
    _bz = Buf("z32s")
    _brp = Buf("rps")
    _bzb = Buf("zbfs")
    b_z32 = [_bz, _bz]
    b_rp = [_brp, _brp]
    b_zbf = [_bzb, _bzb]

    b_x = [Buf("x%d" % c) for c in range(NCH)]
    b_const = Buf("const")
    b_kv32 = Buf("kv32")
    b_ks32 = Buf("ks32")
    b_kvbf = Buf("kvbf")
    b_ksbf = Buf("ksbf")

    def dma(eng_name, out_ap, in_ap, reads=(), writes=(), deps=(), fence=False):
        return OP(eng_name, lambda e, o=out_ap, i=in_ap: e.dma_start(out=o, in_=i),
                  reads=reads, writes=writes, deps=deps, fence=fence, dma=True)

    dma("sync", identb[:, :], identb_h, writes=[b_const])
    dma("sync", identf[:, :], identf_h, writes=[b_const])
    dma("sync", masku4[:, :], masku4_h, writes=[b_const])
    dma("sync", cos_sb[:, :].rearrange("p (c j) -> p c j", j=32),
        cos_h.rearrange("(c p) j -> p c j", p=128), writes=[b_const])
    dma("sync", sin_sb[:, :].rearrange("p (c j) -> p c j", j=32),
        sin_h.rearrange("(c p) j -> p c j", p=128), writes=[b_const])
    dma("sync", onesb[:, :], onesb_h, writes=[b_const])
    dma("sync", esel4[:, :], esel4_h, writes=[b_const])
    dma("sync", e8sel[:, :], e8sel_h, writes=[b_const])
    dma("sync", prefm[:, :], prefm_h, writes=[b_const])
    dma("sync", temp_sb[:, :], temp_h, writes=[b_const])
    dma("sync", csts[:, :], consts_h, writes=[b_const])
    OP("vector", lambda e: e.memset(KVbf[:, :], 0.0), writes=[b_kvbf])
    OP("vector", lambda e: e.memset(ksbf[:, :], 0.0), writes=[b_ksbf])
    dma("gpsimd", x_bf[:, :].rearrange("p (c d) -> p c d", d=D),
        x_in.rearrange("(c p) d -> p c d", p=128), writes=b_x)

    def emit_ln(c, ps_tp, b_pstp, lnset=None):
        L = (lnset or lnP)[c % 2]
        xt = x_bf[:, D * c:D * (c + 1)]
        OP("vector", lambda e: e.reduce_sum(L["m"][:, :], xt, axis=AX),
           reads=[b_x[c]], writes=[L["b"]])
        OP("vector", lambda e: e.tensor_scalar(out=L["m"][:, :], in0=L["m"][:, :],
                                               scalar1=-1.0 / D, scalar2=None, op0=ALU.mult),
           reads=[L["b"]], writes=[L["b"]])
        OP("scalar", lambda e: e.activation(out=L["cen"][:, :], in_=xt,
                                            func=AF.Identity, bias=L["m"][:, :]),
           reads=[b_x[c], L["b"]], writes=[L["b"]])
        OP("scalar", lambda e: e.activation(out=L["sq"][:, :], in_=L["cen"][:, :],
                                            func=AF.Square, accum_out=L["ss"][:, :]),
           reads=[L["b"]], writes=[L["b"], b_rp[0]])
        OP("scalar", lambda e: e.activation(out=L["rs"][:, :], in_=L["ss"][:, :],
                                            func=AF.Sqrt, scale=C_INVD(), bias=C_EPS()),
           reads=[L["b"], b_const], writes=[L["b"]], fence=True)
        OP("vector", lambda e: e.reciprocal(L["rs"][:, :], L["rs"][:, :]),
           reads=[L["b"]], writes=[L["b"]])
        OP("vector", lambda e: e.tensor_scalar(out=L["xnb"][:, :], in0=L["cen"][:, :],
                                               scalar1=L["rs"][:, :], scalar2=None, op0=ALU.mult),
           reads=[L["b"]], writes=[L["b_xnb"]])
        for dc in range(8):
            OP("tensor", lambda e, dc=dc: e.transpose(
                ps_tp[:, 128 * dc:128 * (dc + 1)], L["xnb"][:, 128 * dc:128 * (dc + 1)],
                identb[:, :]),
               reads=[L["b_xnb"], b_const], writes=[b_pstp])
        OP("vector", lambda e: e.tensor_copy(out=L["xnfm"][:, :], in_=ps_tp[:, 0:D]),
           reads=[b_pstp], writes=[L["b_xnfm"]])
        return L

    def emit_ropefeat(c, zbf_out, b_out, rope=None):
        p = c % 2
        z32, za, zb, b_z, b_s = (rope or (z32P[p], rpaP[p], rpbP[p],
                                          b_z32[p], b_rp[p]))

        def v3(t, off):
            return t[:, :].rearrange("p (h d) -> p h d", d=DH)[:, :, off:off + 32]

        cs = cos_sb[:, :].rearrange("p (cc j) -> p cc j", j=32)[:, c:c + 1, :]
        sn = sin_sb[:, :].rearrange("p (cc j) -> p cc j", j=32)[:, c:c + 1, :]
        za3 = za[:, 0:512].rearrange("p (h j) -> p h j", j=32)
        zb3 = zb[:, 0:512].rearrange("p (h j) -> p h j", j=32)

        def tt3(e, out_ap, a_ap, b_ap, op):
            aa, bb = bass.broadcast_tensor_aps(a_ap, b_ap)
            return e.tensor_tensor(out=out_ap, in0=aa, in1=bb, op=op)

        q1 = v3(z32, 0)
        q2 = v3(z32, 32)
        OP("vector", lambda e: tt3(e, za3, q1, cs, ALU.mult),
           reads=[b_z, b_const], writes=[b_s])
        OP("vector", lambda e: tt3(e, zb3, q2, sn, ALU.mult),
           reads=[b_z, b_const], writes=[b_s])
        OP("vector", lambda e: e.tensor_tensor(out=za[:, 0:512], in0=za[:, 0:512],
                                               in1=zb[:, 0:512], op=ALU.subtract),
           reads=[b_s], writes=[b_s])
        OP("vector", lambda e: tt3(e, zb3, q1, sn, ALU.mult),
           reads=[b_z, b_const], writes=[b_s])
        OP("vector", lambda e: tt3(e, v3(z32, 32), v3(z32, 32), cs, ALU.mult),
           reads=[b_z, b_const], writes=[b_z])
        OP("vector", lambda e: e.tensor_tensor(
            out=v3(z32, 32), in0=v3(z32, 32), in1=zb3, op=ALU.add),
           reads=[b_z, b_s], writes=[b_z])
        OP("vector", lambda e: e.tensor_copy(out=v3(z32, 0), in_=za3),
           reads=[b_s], writes=[b_z])
        OP("vector", lambda e: e.tensor_scalar(out=za[:, 0:D], in0=z32[:, :],
                                               scalar1=0.0, scalar2=None, op0=ALU.min),
           reads=[b_z], writes=[b_s])
        OP("scalar", lambda e: e.activation(out=zb[:, 0:D], in_=za[:, 0:D],
                                            func=AF.Exp, scale=C_SCALE()),
           reads=[b_s, b_const], writes=[b_s])
        OP("scalar", lambda e: e.activation(out=za[:, 0:D], in_=z32[:, :],
                                            func=AF.Relu, scale=C_SCALE()),
           reads=[b_z, b_const], writes=[b_s])
        OP("vector", lambda e: e.tensor_tensor(out=zbf_out, in0=za[:, 0:D],
                                               in1=zb[:, 0:D], op=ALU.add),
           reads=[b_s], writes=[b_out])

    p12 = ExitStack()
    k_tm = p12.enter_context(nc.sbuf_tensor("k_tm", [128, NCH * D], BF16))
    v_tm = p12.enter_context(nc.sbuf_tensor("v_tm", [128, NCH * D], BF16))
    b_ktm = [Buf("ktm%d" % c) for c in range(NCH)]
    b_vtm = [Buf("vtm%d" % c) for c in range(NCH)]

    with ExitStack() as p1:
        sb1 = lambda name, shape, dt: p1.enter_context(nc.sbuf_tensor(name, shape, dt))
        ps1 = lambda name, shape, dt: p1.enter_context(nc.psum_tensor(name, shape, dt))
        wkv_bf = sb1("wkv_bf", [128, 8 * 2048], BF16)
        ln1P = []
        for p in range(2):
            d = dict(cen=sb1("p1cen%d" % p, [128, D], F32),
                     sq=sb1("p1sq%d" % p, [128, D], F32),
                     m=sb1("p1m%d" % p, [128, 1], F32),
                     ss=sb1("p1ss%d" % p, [128, 1], F32),
                     rs=sb1("p1rs%d" % p, [128, 1], F32),
                     xnb=sb1("p1xnb%d" % p, [128, D], BF16),
                     xnfm=sb1("p1xnfm%d" % p, [128, D], BF16))
            d["b"] = Buf("p1ln%d" % p)
            d["b_xnb"] = Buf("p1xnb%d" % p)
            d["b_xnfm"] = Buf("p1xnfm%d" % p)
            ln1P.append(d)
        rope1P = []
        for p in range(2):
            rope1P.append((sb1("p1z32%d" % p, [128, D], F32),
                           sb1("p1rpa%d" % p, [128, D], F32),
                           sb1("p1rpb%d" % p, [128, D], BF16),
                           Buf("p1z%d" % p), Buf("p1rp%d" % p)))
        ps_tp1 = ps1("ps_tp1", [128, 1024], BF16)
        ps_kvpP = [ps1("ps_kvp%d" % p, [128, 1024], F32) for p in range(2)]
        ps_kv1 = ps1("ps_kv1", [64, 1024], F32)
        ps_ks1 = ps1("ps_ks1", [64, 16], F32)
        b_wkv = Buf("wkv")
        b_tp1 = Buf("tp1")
        b_kvp = [Buf("kvp%d" % p) for p in range(2)]
        b_pkv = Buf("pkv")
        b_kv16 = Buf("kv16")
        b_pks = Buf("pks")

        dma("gpsimd", wkv_bf[:, :].rearrange("p (dc n) -> p dc n", n=2048),
            wqkv.rearrange("(dc p) n -> p dc n", p=128)[:, :, D:3 * D],
            writes=[b_wkv])

        for c in range(NCH):
            par = c % 2
            L = emit_ln(c, ps_tp1, b_tp1, lnset=ln1P)
            ps_kvp = ps_kvpP[par]
            b_kp = b_kvp[par]
            for which in ("k", "v"):
                base = 0 if which == "k" else 1024
                for j in range(2):
                    for dc in range(8):
                        OP("tensor", lambda e, j=j, dc=dc, base=base, L=L, ps_kvp=ps_kvp: e.matmul(
                            ps_kvp[:, 512 * j:512 * (j + 1)],
                            L["xnfm"][:, 128 * dc:128 * (dc + 1)],
                            wkv_bf[:, 2048 * dc + base + 512 * j: 2048 * dc + base + 512 * (j + 1)],
                            start=(dc == 0), stop=(dc == 7)),
                           reads=[L["b_xnfm"], b_wkv], writes=[b_kp])
                if which == "k":
                    OP("scalar", lambda e, par=par, ps_kvp=ps_kvp: e.copy(
                        out=rope1P[par][0][:, :], in_=ps_kvp[:, 0:D]),
                       reads=[b_kp], writes=[rope1P[par][3]])
                else:
                    OP("vector", lambda e, c=c, ps_kvp=ps_kvp: e.tensor_copy(
                        out=v_tm[:, D * c:D * (c + 1)], in_=ps_kvp[:, 0:D]),
                       reads=[b_kp], writes=[b_vtm[c]])
            emit_ropefeat(c, k_tm[:, D * c:D * (c + 1)], b_ktm[c],
                          rope=rope1P[par])
            for h in range(H):
                OP("tensor", lambda e, c=c, h=h: e.matmul(
                    ps_kv1[:, 64 * h:64 * (h + 1)],
                    k_tm[:, D * c + 64 * h: D * c + 64 * (h + 1)],
                    v_tm[:, D * c + 64 * h: D * c + 64 * (h + 1)],
                    start=(c == 0 and (h == 0 or h == 8)), stop=(c == NCH - 1)),
                   reads=[b_ktm[c], b_vtm[c]], writes=[b_pkv])
            for h in range(H):
                OP("tensor", lambda e, c=c, h=h: e.matmul(
                    ps_ks1[:, h:h + 1],
                    k_tm[:, D * c + 64 * h: D * c + 64 * (h + 1)],
                    onesb[:, :],
                    start=(c == 0 and h == 0), stop=(c == NCH - 1)),
                   reads=[b_ktm[c], b_const], writes=[b_pks])

        for c0 in (0, 1):
            emit_ln(c0, ps_tp1, b_tp1)
        pb.barrier()

    with ExitStack() as pex:
        slabs = pex.enter_context(nc.sbuf_tensor("slabs", [64, 8 * 1040], BF16))
        slab32 = pex.enter_context(nc.sbuf_tensor("slab32", [64, 1040], F32))
        kvoff = pex.enter_context(nc.sbuf_tensor("kvoff", [64, 1040], F32))
        kv16 = pex.enter_context(nc.sbuf_tensor("kv16", [64, 1040], BF16))
        OP("scalar", lambda e: e.copy(out=kv16[:, 0:1024], in_=ps_kv1[:, :]),
           reads=[b_pkv], writes=[b_kv16])
        OP("scalar", lambda e: e.copy(out=kv16[:, 1024:1040], in_=ps_ks1[:, :]),
           reads=[b_pks], writes=[b_kv16])
        t1 = dma("sync", kv_loc.ap()[:, :], kv16[:, :], reads=[b_kv16])
        tok_cc = OP("gpsimd", lambda e: e.collective_compute(
            "AllGather", ALU.bypass, replica_groups=[list(range(NCORES))],
            ins=[kv_loc.ap()], outs=[kv_all.ap()]),
            deps=[t1], is_async=True)
        b_slab = Buf("slabs")
        b_s32 = Buf("slab32")
        b_kvoff = Buf("kvoff")
        dma("gpsimd", slabs[:, :].rearrange("p (r n) -> p r n", n=1040),
            kv_all.ap().rearrange("(r p) n -> p r n", p=64),
            deps=[tok_cc], writes=[b_slab])
        for j in range(NCORES):
            OP("vector", lambda e, j=j: e.tensor_copy(
                out=slab32[:, :], in_=slabs[:, 1040 * j:1040 * (j + 1)]),
               reads=[b_slab], writes=[b_s32])
            if j == 0:
                OP("vector", lambda e: e.tensor_scalar(
                    out=kvoff[:, :], in0=slab32[:, :],
                    scalar1=prefm[:, 0:1], scalar2=None, op0=ALU.mult),
                   reads=[b_s32, b_const], writes=[b_kvoff])
            else:
                OP("vector", lambda e, j=j: e.scalar_tensor_tensor(
                    out=kvoff[:, :], in0=slab32[:, :],
                    scalar=prefm[:, j:j + 1], in1=kvoff[:, :],
                    op0=ALU.mult, op1=ALU.add),
                   reads=[b_s32, b_const, b_kvoff], writes=[b_kvoff])
        OP("vector", lambda e: e.tensor_copy(out=KV32[:, :], in_=kvoff[:, 0:1024]),
           reads=[b_kvoff], writes=[b_kv32])
        OP("vector", lambda e: e.tensor_copy(out=ks32[:, :], in_=kvoff[:, 1024:1040]),
           reads=[b_kvoff], writes=[b_ks32])
        OP("vector", lambda e: e.tensor_copy(out=KVbf[0:64, :], in_=KV32[:, :]),
           reads=[b_kv32], writes=[b_kvbf])
        OP("vector", lambda e: e.tensor_copy(out=ksbf[0:64, :], in_=ks32[:, :]),
           reads=[b_ks32], writes=[b_ksbf])
        if debug:
            dma("sync", dbg["d_kv"], KV32[:, :], reads=[b_kv32])
            dma("sync", dbg["d_ks"], ks32[:, :], reads=[b_ks32])
        pb.barrier()

    with ExitStack() as p2:
        sb2 = lambda name, shape, dt: p2.enter_context(nc.sbuf_tensor(name, shape, dt))
        ps2 = lambda name, shape, dt: p2.enter_context(nc.psum_tensor(name, shape, dt))
        wq_bf = sb2("wq_bf", [128, 8 * 1024], BF16)
        wo_bf = sb2("wo_bf", [128, 16 * 1024], BF16)
        qfmP = [sb2("qfm%d" % p, [128, H * 128], BF16) for p in range(2)]
        _kfm = sb2("kfms", [64, H * 128], BF16)
        kfmP = [_kfm, _kfm]
        smask = sb2("smask", [128, 512], BF16)
        denr = sb2("denr", [128, 16], F32)
        denA = sb2("denA", [128, 4], F32)
        rT = sb2("rT", [4, 128], BF16)
        divs = sb2("divs", [64, 512], F32)
        attn_c = sb2("attn_c", [128, H * 128], BF16)
        o1_bf = sb2("o1_bf", [128, D], BF16)

        ps_tp2P = [ps2("ps_tp2%d" % p, [128, 1024], BF16) for p in range(2)]
        ps_big = ps2("ps_big", [128, 512], F32)
        ps_S = ps2("ps_S", [128, 512], F32)
        ps_dr = ps2("ps_dr", [128, 256], F32)
        ps_OP = [ps2("ps_O%d" % p, [64, 512], F32) for p in range(2)]
        ps_o1 = ps2("ps_o1", [128, 512], F32)

        b_wq = Buf("wq")
        b_wo = Buf("wo")
        b_qfm = [Buf("qfm%d" % p) for p in range(2)]
        _bkfm = Buf("kfms")
        b_kfm = [_bkfm, _bkfm]
        b_sm = Buf("smask")
        b_denr = Buf("denr")
        b_rT = Buf("rT")
        b_div = Buf("divs")
        b_attn = Buf("attn")
        b_o1 = Buf("o1")
        b_tp2 = [Buf("tp2%d" % p) for p in range(2)]
        b_big = Buf("big")
        b_psS = Buf("psS")
        b_pdr = Buf("pdr")
        b_psOP = [Buf("psO0"), Buf("psO1")]
        b_po1 = Buf("po1")

        OP("vector", lambda e: e.memset(qfmP[0][:, :], 0.0), writes=[b_qfm[0]])
        OP("vector", lambda e: e.memset(qfmP[1][:, :], 0.0), writes=[b_qfm[1]])
        OP("vector", lambda e: e.memset(attn_c[:, :], 0.0), writes=[b_attn])
        OP("vector", lambda e: e.memset(denr[:, :], 0.0), writes=[b_denr])
        OP("vector", lambda e: e.memset(wo_bf[:, :], 0.0), writes=[b_wo])
        dma("gpsimd", wq_bf[:, :].rearrange("p (dc n) -> p dc n", n=1024),
            wqkv.rearrange("(dc p) n -> p dc n", p=128)[:, :, 0:D], writes=[b_wq])
        dma("gpsimd", wo_bf[0:64, :].rearrange("p (h n) -> p h n", n=1024),
            wout.rearrange("(h p) n -> p h n", p=64), writes=[b_wo])

        for c in range(NCH):
            par = c % 2
            qfm = qfmP[par]
            kfm = kfmP[par]
            bq = b_qfm[par]
            bk = b_kfm[par]
            ps_tp2 = ps_tp2P[par]
            btp = b_tp2[par]
            L = lnP[par] if c < 2 else emit_ln(c, ps_tp2, btp)
            for j in range(2):
                for dc in range(8):
                    OP("tensor", lambda e, j=j, dc=dc, L=L: e.matmul(
                        ps_big[:, :],
                        L["xnfm"][:, 128 * dc:128 * (dc + 1)],
                        wq_bf[:, 1024 * dc + 512 * j: 1024 * dc + 512 * (j + 1)],
                        start=(dc == 0), stop=(dc == 7)),
                       reads=[L["b_xnfm"], b_wq], writes=[b_big])
                OP("scalar", lambda e, par=par, j=j: e.copy(
                    out=z32P[par][:, 512 * j:512 * (j + 1)], in_=ps_big[:, :]),
                   reads=[b_big], writes=[b_z32[par]])
            emit_ropefeat(c, zbfP[par][:, :], b_zbf[par])
            for src_bf, b_src, dst, b_dst in (
                    (zbfP[par][:, :], b_zbf[par], qfm, bq),
                    (k_tm[:, D * c:D * (c + 1)], b_ktm[c], kfm, bk)):
                for rnd in range(2):
                    for hh in range(8):
                        h = rnd * 8 + hh
                        OP("tensor", lambda e, h=h, hh=hh, s=src_bf, ps_tp2=ps_tp2: e.transpose(
                            ps_tp2[0:64, 128 * hh:128 * (hh + 1)],
                            s[:, 64 * h:64 * (h + 1)], identb[:, :]),
                           reads=[b_src, b_const], writes=[btp])
                    OP("vector", lambda e, rnd=rnd, d=dst, ps_tp2=ps_tp2: e.tensor_copy(
                        out=d[0:64, 1024 * rnd:1024 * (rnd + 1)], in_=ps_tp2[0:64, :]),
                       reads=[btp], writes=[b_dst])
            for g in range(4):
                for jj in range(4):
                    h = 4 * g + jj
                    OP("tensor", lambda e, h=h, jj=jj, qfm=qfm, kfm=kfm: e.matmul(
                        ps_S[:, 128 * jj:128 * (jj + 1)],
                        kfm[0:64, 128 * h:128 * (h + 1)],
                        qfm[0:64, 128 * h:128 * (h + 1)],
                        start=True, stop=True),
                       reads=[bk, bq], writes=[b_psS])
                OP("vector", lambda e: e.tensor_tensor(
                    out=smask[:, :], in0=ps_S[:, :], in1=masku4[:, :], op=ALU.mult),
                   reads=[b_psS, b_const], writes=[b_sm])
                for jj in range(4):
                    OP("tensor", lambda e, jj=jj: e.matmul(
                        ps_dr[:, jj:jj + 1],
                        smask[:, 128 * jj:128 * (jj + 1)], onesb[:, :],
                        start=True, stop=True),
                       reads=[b_sm, b_const], writes=[b_pdr])
                for jj in range(4):
                    h = 4 * g + jj
                    OP("tensor", lambda e, h=h, jj=jj, qfm=qfm: e.matmul(
                        ps_dr[:, 64 + jj:64 + jj + 1],
                        qfm[:, 128 * h:128 * (h + 1)], ksbf[:, h:h + 1],
                        start=True, stop=True),
                       reads=[bq, b_ksbf], writes=[b_pdr])
                ps_O = ps_OP[g % 2]
                b_psO = b_psOP[g % 2]
                for jj in range(4):
                    h = 4 * g + jj
                    OP("tensor", lambda e, h=h, jj=jj, c=c, ps_O=ps_O: e.matmul(
                        ps_O[:, 128 * jj:128 * (jj + 1)],
                        v_tm[:, D * c + 64 * h: D * c + 64 * (h + 1)],
                        smask[:, 128 * jj:128 * (jj + 1)],
                        start=True, stop=False),
                       reads=[b_vtm[c], b_sm], writes=[b_psO])
                    OP("tensor", lambda e, h=h, jj=jj, qfm=qfm, ps_O=ps_O: e.matmul(
                        ps_O[:, 128 * jj:128 * (jj + 1)],
                        KVbf[:, 64 * h:64 * (h + 1)],
                        qfm[:, 128 * h:128 * (h + 1)],
                        start=False, stop=True),
                       reads=[b_kvbf, bq], writes=[b_psO])
                OP("vector", lambda e: e.tensor_copy(
                    out=denA[:, :], in_=ps_dr[:, 0:4]),
                   reads=[b_pdr], writes=[b_denr])
                OP("vector", lambda e: e.tensor_tensor(
                    out=denA[:, :], in0=denA[:, :], in1=ps_dr[:, 64:68], op=ALU.add),
                   reads=[b_pdr, b_denr], writes=[b_denr])
                OP("vector", lambda e: e.tensor_scalar(
                    out=denr[:, 0:4], in0=denA[:, :],
                    scalar1=1e-6, scalar2=None, op0=ALU.max),
                   reads=[b_denr], writes=[b_denr])
                OP("vector", lambda e: e.reciprocal(denr[:, 0:4], denr[:, 0:4]),
                   reads=[b_denr], writes=[b_denr])
                OP("tensor", lambda e: e.transpose(
                    ps_dr[0:16, 128:256], denr[:, :], identf[:, :]),
                   reads=[b_denr, b_const], writes=[b_pdr])
                OP("vector", lambda e: e.tensor_copy(out=rT[:, :], in_=ps_dr[0:4, 128:256]),
                   reads=[b_pdr], writes=[b_rT])
                for jj in range(4):
                    OP("tensor", lambda e, jj=jj: e.matmul(
                        ps_o1[0:64, 128 * jj:128 * (jj + 1)],
                        esel4[:, 64 * jj:64 * (jj + 1)], rT[:, :],
                        start=True, stop=True),
                       reads=[b_rT, b_const], writes=[b_po1])
                OP("vector", lambda e: e.tensor_copy(out=divs[:, :], in_=ps_o1[0:64, :]),
                   reads=[b_po1], writes=[b_div])
                OP("vector", lambda e, g=g, ps_O=ps_O: e.tensor_tensor(
                    out=attn_c[0:64, 512 * g:512 * (g + 1)], in0=ps_O[:, :],
                    in1=divs[:, :], op=ALU.mult),
                   reads=[b_psO, b_div], writes=[b_attn])
            for rr in range(2):
                for hh in range(8):
                    h = rr * 8 + hh
                    OP("tensor", lambda e, h=h, hh=hh, c=c: e.matmul(
                        ps_big[0:64, 64 * hh:64 * (hh + 1)],
                        k_tm[:, D * c + 64 * h: D * c + 64 * (h + 1)],
                        v_tm[:, D * c + 64 * h: D * c + 64 * (h + 1)],
                        start=(hh == 0), stop=(hh == 7)),
                       reads=[b_ktm[c], b_vtm[c]], writes=[b_big])
                OP("vector", lambda e, rr=rr: e.tensor_tensor(
                    out=KV32[:, 512 * rr:512 * (rr + 1)],
                    in0=KV32[:, 512 * rr:512 * (rr + 1)],
                    in1=ps_big[0:64, :], op=ALU.add),
                   reads=[b_big, b_kv32], writes=[b_kv32])
            for h in range(H):
                OP("tensor", lambda e, h=h, c=c: e.matmul(
                    ps_dr[0:64, 32 + h:33 + h],
                    k_tm[:, D * c + 64 * h: D * c + 64 * (h + 1)], onesb[:, :],
                    start=(h == 0), stop=(h == 15)),
                   reads=[b_ktm[c], b_const], writes=[b_pdr])
            OP("vector", lambda e: e.tensor_tensor(
                out=ks32[:, :], in0=ks32[:, :], in1=ps_dr[0:64, 32:48], op=ALU.add),
               reads=[b_pdr, b_ks32], writes=[b_ks32])
            OP("scalar", lambda e: e.copy(out=KVbf[0:64, :], in_=KV32[:, :]),
               reads=[b_kv32], writes=[b_kvbf])
            OP("scalar", lambda e: e.copy(out=ksbf[0:64, :], in_=ks32[:, :]),
               reads=[b_ks32], writes=[b_ksbf])
            for rnd in range(2):
                for dsl in range(4):
                    ds = rnd * 4 + dsl
                    for h in range(H):
                        OP("tensor", lambda e, ds=ds, dsl=dsl, h=h: e.matmul(
                            ps_o1[:, 128 * dsl:128 * (dsl + 1)],
                            wo_bf[:, 1024 * h + 128 * ds: 1024 * h + 128 * (ds + 1)],
                            attn_c[:, 128 * h:128 * (h + 1)],
                            start=(h == 0), stop=(h == H - 1)),
                           reads=[b_wo, b_attn], writes=[b_po1])
                OP("vector", lambda e, rnd=rnd: e.tensor_copy(
                    out=o1_bf[:, 512 * rnd:512 * (rnd + 1)], in_=ps_o1[:, :]),
                   reads=[b_po1], writes=[b_o1])
            for ds in range(8):
                OP("tensor", lambda e, ds=ds, ps_tp2=ps_tp2: e.transpose(
                    ps_tp2[:, 128 * ds:128 * (ds + 1)],
                    o1_bf[:, 128 * ds:128 * (ds + 1)], identb[:, :]),
                   reads=[b_o1, b_const], writes=[btp])
            OP("vector", lambda e, c=c, ps_tp2=ps_tp2: e.tensor_tensor(
                out=x_bf[:, D * c:D * (c + 1)], in0=x_bf[:, D * c:D * (c + 1)],
                in1=ps_tp2[:, :], op=ALU.add),
               reads=[btp], writes=[b_x[c]])
            if debug and c == 0:
                dma("gpsimd", dbg["d_attn"], attn_c[0:64, :], reads=[b_attn])
                dma("gpsimd", dbg["d_x1"], x_bf[:, 0:D], reads=[b_x[0]])
        pb.barrier()

    p12.close()

    with ExitStack() as p3:
        sb3 = lambda name, shape, dt: p3.enter_context(nc.sbuf_tensor(name, shape, dt))
        ps3 = lambda name, shape, dt: p3.enter_context(nc.psum_tensor(name, shape, dt))
        wr_bf = sb3("wr_bf", [128, 8 * 8], BF16)
        wT_sbP = [sb3("wT_sb%d" % p, [8, 1024], BF16) for p in range(2)]
        wg_sb = sb3("wg_sb", [128, 8192], BF16)
        wu_sb = sb3("wu_sb", [128, 8192], BF16)
        wd_sb = sb3("wd_sb", [128, 8192], BF16)
        xn2fmP = [sb3("xn2fm%d" % p, [128, 8192], BF16) for p in range(2)]
        h_all = sb3("h_all", [128, 8192], BF16)
        acc = sb3("acc", [128, 8192], F32)
        wb_sb = sb3("wb_sb", [128, 1024], BF16)
        sg_sbP = [sb3("sg_sb%d" % p, [128, 512], BF16) for p in range(2)]
        hp_sb1 = sb3("hp_sb1", [128, 512], BF16)
        hp_sbP = [hp_sb1, hp_sb1]
        out_st = sb3("out_st", [128, 512], F32)
        rt = {nm: sb3("rt_" + nm, [128, 32], F32) for nm in
              ("e", "inh", "z1", "z2", "zone", "dist", "w", "fb", "m1x", "fb2",
               "fbw", "wm", "wfin")}
        rs = {nm: sb3("rs_" + nm, [128, 4], F32) for nm in
              ("wsum", "nex", "nex1", "m1", "m2", "fbs", "rec", "ws2", "rec2")}
        ntr_sb = sb3("ntr_sb", [128, 1], F32)
        rs["ntr"] = ntr_sb
        b_ntr = Buf("ntr")

        ps_tp3 = ps3("ps_tp3", [128, 1024], BF16)
        ps_guP = [ps3("ps_gu%d" % p, [128, 1024], F32) for p in range(2)]
        ps_dn = ps3("ps_dn", [128, 1024], F32)
        ps_ft = ps3("ps_ft", [128, 512], F32)

        b_wr = Buf("wr")
        b_wT = [Buf("wT%d" % p) for p in range(2)]
        b_wg = Buf("wg")
        b_wu = Buf("wu")
        b_wd = Buf("wd")
        b_xn2 = [Buf("xn2_%d" % p) for p in range(2)]
        b_h = Buf("h_all")
        b_acc = Buf("acc")
        b_wb = Buf("wb")
        b_sgP = [Buf("sg0"), Buf("sg1")]
        b_hp1 = Buf("hp1")
        b_hpP = [b_hp1, b_hp1]
        b_ost = Buf("ost")
        b_rt = Buf("rt")
        b_tp3 = Buf("tp3")
        b_gu = [Buf("gu%d" % p) for p in range(2)]
        b_pdnH = [Buf("pdn0"), Buf("pdn1")]
        b_pft = Buf("pft")

        dma("gpsimd", wr_bf[:, :].rearrange("p (dc n) -> p dc n", n=8),
            wr_h.rearrange("(dc p) n -> p dc n", p=128), writes=[b_wr])
        OP("vector", lambda e: e.reciprocal(rs["ntr"][:, :], temp_sb[:, :]),
           reads=[b_const], writes=[b_ntr])
        OP("vector", lambda e: e.tensor_scalar(
            out=rs["ntr"][:, :], in0=rs["ntr"][:, :], scalar1=-1.0, scalar2=None,
            op0=ALU.mult), reads=[b_ntr], writes=[b_ntr])

        def emit_routing_batch(bi):
            cs0 = bi * 4
            hf = cs0 // 8
            xn2fm = xn2fmP[hf]
            for ci in range(4):
                c = cs0 + ci
                tc = c % 8
                L = emit_ln(c, ps_tp3, b_tp3)
                OP("vector", lambda e, tc=tc, xn2fm=xn2fm: e.tensor_copy(
                    out=xn2fm[:, :].rearrange("p (dc n) -> p dc n", n=1024)[:, :, 128 * tc:128 * (tc + 1)],
                    in_=ps_tp3[:, :].rearrange("p (dc n) -> p dc n", n=128)),
                   reads=[b_tp3], writes=[b_xn2[hf]])
                for dc in range(8):
                    OP("tensor", lambda e, dc=dc, ci=ci, L=L: e.matmul(
                        ps_ft[:, 8 * ci:8 * (ci + 1)],
                        L["xnfm"][:, 128 * dc:128 * (dc + 1)],
                        wr_bf[:, 8 * dc:8 * (dc + 1)],
                        start=(dc == 0), stop=(dc == 7)),
                       reads=[L["b_xnfm"], b_wr], writes=[b_pft])
            V, S = "vector", "scalar"

            def t3(t):
                return t[:, :].rearrange("p (c e) -> p c e", e=8)

            def s3(t):
                return t[:, :].rearrange("p (c e) -> p c e", e=1)

            def btt(e, out_ap, a_ap, b_ap, op):
                aa, bb = bass.broadcast_tensor_aps(a_ap, b_ap)
                return e.tensor_tensor(out=out_ap, in0=aa, in1=bb, op=op)

            OP(S, lambda e: e.activation(out=rt["e"][:, :], in_=ps_ft[:, 0:32],
                                         func=AF.Exp, scale=rs["ntr"][:, :]),
               reads=[b_pft, b_ntr], writes=[b_rt])
            OP(V, lambda e: e.tensor_scalar(out=rt["inh"][:, :], in0=rt["e"][:, :],
                                            scalar1=1.0, scalar2=None, op0=ALU.add),
               reads=[b_rt], writes=[b_rt])
            OP(V, lambda e: e.reciprocal(rt["inh"][:, :], rt["inh"][:, :]),
               reads=[b_rt], writes=[b_rt])
            OP(V, lambda e: e.tensor_single_scalar(out=rt["z1"][:, :], in_=rt["inh"][:, :],
                                                   scalar=GL, op=ALU.is_ge),
               reads=[b_rt], writes=[b_rt])
            OP(V, lambda e: e.tensor_single_scalar(out=rt["z2"][:, :], in_=rt["inh"][:, :],
                                                   scalar=GU, op=ALU.is_le),
               reads=[b_rt], writes=[b_rt])
            OP(V, lambda e: e.tensor_tensor(out=rt["zone"][:, :], in0=rt["z1"][:, :],
                                            in1=rt["z2"][:, :], op=ALU.mult),
               reads=[b_rt], writes=[b_rt])
            OP(V, lambda e: e.tensor_scalar(out=rt["dist"][:, :], in0=rt["inh"][:, :],
                                            scalar1=GC, scalar2=None, op0=ALU.subtract),
               reads=[b_rt], writes=[b_rt])
            OP(S, lambda e: e.activation(out=rt["dist"][:, :], in_=rt["dist"][:, :],
                                         func=AF.Abs),
               reads=[b_rt], writes=[b_rt])
            OP(S, lambda e: e.activation(out=rt["w"][:, :], in_=rt["dist"][:, :],
                                         func=AF.Exp, scale=C_N10()),
               reads=[b_rt, b_const], writes=[b_rt])
            OP(V, lambda e: e.tensor_tensor(out=rt["w"][:, :], in0=rt["w"][:, :],
                                            in1=rt["zone"][:, :], op=ALU.mult),
               reads=[b_rt], writes=[b_rt])
            OP(V, lambda e: e.reduce_sum(s3(rs["wsum"]), t3(rt["w"]), axis=AX),
               reads=[b_rt], writes=[b_rt])
            OP(V, lambda e: e.tensor_single_scalar(out=rs["nex"][:, :], in_=rs["wsum"][:, :],
                                                   scalar=1e-8, op=ALU.is_lt),
               reads=[b_rt], writes=[b_rt])
            OP(S, lambda e: e.activation(out=rt["fb"][:, :], in_=rt["dist"][:, :],
                                         func=AF.Exp, scale=C_N103()),
               reads=[b_rt, b_const], writes=[b_rt])
            OP(V, lambda e: e.reduce_max(s3(rs["m1"]), t3(rt["fb"]), axis=AX),
               reads=[b_rt], writes=[b_rt])
            OP(V, lambda e: btt(e, t3(rt["m1x"]), t3(rt["fb"]), s3(rs["m1"]), ALU.is_ge),
               reads=[b_rt], writes=[b_rt])
            OP(V, lambda e: e.tensor_scalar(out=rt["fb2"][:, :], in0=rt["fb"][:, :],
                                            scalar1=1.0, scalar2=None, op0=ALU.add),
               reads=[b_rt], writes=[b_rt])
            OP(V, lambda e: e.tensor_tensor(out=rt["fb2"][:, :], in0=rt["fb2"][:, :],
                                            in1=rt["m1x"][:, :], op=ALU.mult),
               reads=[b_rt], writes=[b_rt])
            OP(V, lambda e: e.tensor_tensor(out=rt["fb2"][:, :], in0=rt["fb"][:, :],
                                            in1=rt["fb2"][:, :], op=ALU.subtract),
               reads=[b_rt], writes=[b_rt])
            OP(V, lambda e: e.reduce_max(s3(rs["m2"]), t3(rt["fb2"]), axis=AX),
               reads=[b_rt], writes=[b_rt])
            OP(V, lambda e: btt(e, t3(rt["m1x"]), t3(rt["fb"]), s3(rs["m2"]), ALU.is_ge),
               reads=[b_rt], writes=[b_rt])
            OP(V, lambda e: e.tensor_tensor(out=rt["fbw"][:, :], in0=rt["fb"][:, :],
                                            in1=rt["m1x"][:, :], op=ALU.mult),
               reads=[b_rt], writes=[b_rt])
            OP(V, lambda e: e.reduce_sum(s3(rs["fbs"]), t3(rt["fbw"]), axis=AX),
               reads=[b_rt], writes=[b_rt])
            OP(V, lambda e: e.tensor_single_scalar(out=rs["fbs"][:, :], in_=rs["fbs"][:, :],
                                                   scalar=1e-8, op=ALU.max),
               reads=[b_rt], writes=[b_rt])
            OP(V, lambda e: e.reciprocal(rs["rec"][:, :], rs["fbs"][:, :]),
               reads=[b_rt], writes=[b_rt])
            OP(V, lambda e: btt(e, t3(rt["fbw"]), t3(rt["fbw"]), s3(rs["rec"]), ALU.mult),
               reads=[b_rt], writes=[b_rt])
            OP(V, lambda e: e.tensor_scalar(out=rs["nex1"][:, :], in0=rs["nex"][:, :],
                                            scalar1=-1.0, scalar2=1.0, op0=ALU.mult, op1=ALU.add),
               reads=[b_rt], writes=[b_rt])
            OP(V, lambda e: btt(e, t3(rt["w"]), t3(rt["w"]), s3(rs["nex1"]), ALU.mult),
               reads=[b_rt], writes=[b_rt])
            OP(V, lambda e: btt(e, t3(rt["fbw"]), t3(rt["fbw"]), s3(rs["nex"]), ALU.mult),
               reads=[b_rt], writes=[b_rt])
            OP(V, lambda e: e.tensor_tensor(out=rt["wm"][:, :], in0=rt["w"][:, :],
                                            in1=rt["fbw"][:, :], op=ALU.add),
               reads=[b_rt], writes=[b_rt])
            OP(V, lambda e: e.reduce_sum(s3(rs["ws2"]), t3(rt["wm"]), axis=AX),
               reads=[b_rt], writes=[b_rt])
            OP(V, lambda e: e.tensor_single_scalar(out=rs["ws2"][:, :], in_=rs["ws2"][:, :],
                                                   scalar=1e-8, op=ALU.max),
               reads=[b_rt], writes=[b_rt])
            OP(V, lambda e: e.reciprocal(rs["rec2"][:, :], rs["ws2"][:, :]),
               reads=[b_rt], writes=[b_rt])
            OP(V, lambda e: btt(e, t3(rt["wfin"]), t3(rt["wm"]), s3(rs["rec2"]), ALU.mult),
               reads=[b_rt], writes=[b_rt])
            for ci in range(4):
                c = cs0 + ci
                tc = c % 8
                OP("tensor", lambda e, ci=ci: e.transpose(
                    ps_ft[0:8, 128:256], rt["wfin"][:, 8 * ci:8 * (ci + 1)],
                    identf[:, :]),
                   reads=[b_rt, b_const], writes=[b_pft])
                OP("vector", lambda e, tc=tc, hf=hf: e.tensor_copy(
                    out=wT_sbP[hf][:, 128 * tc:128 * (tc + 1)], in_=ps_ft[0:8, 128:256]),
                   reads=[b_pft], writes=[b_wT[hf]])

        for hf in range(2):
            for bi in (2 * hf, 2 * hf + 1):
                emit_routing_batch(bi)
            xn2fm = xn2fmP[hf]
            for ei in range(E):
                dma("gpsimd", wg_sb[:, :].rearrange("p (dc n) -> p dc n", n=1024),
                    wg_h[ei].rearrange("(dc p) n -> p dc n", p=128), writes=[b_wg])
                dma("gpsimd", wu_sb[:, :].rearrange("p (dc n) -> p dc n", n=1024),
                    wu_h[ei].rearrange("(dc p) n -> p dc n", p=128), writes=[b_wu])
                dma("gpsimd", wd_sb[:, :].rearrange("p (ic n) -> p ic n", n=1024),
                    wd_h[ei].rearrange("(ic p) n -> p ic n", p=128), writes=[b_wd])
                for j in range(2):
                    OP("tensor", lambda e, j=j, ei=ei, hf=hf: e.matmul(
                        ps_ft[:, :],
                        e8sel[:, 128 * ei:128 * (ei + 1)],
                        wT_sbP[hf][:, 512 * j:512 * (j + 1)],
                        start=True, stop=True),
                       reads=[b_wT[hf], b_const], writes=[b_pft])
                    OP("vector", lambda e, j=j: e.tensor_copy(
                        out=wb_sb[:, 512 * j:512 * (j + 1)], in_=ps_ft[:, :]),
                       reads=[b_pft], writes=[b_wb])
                for ic in range(8):
                    for tt in range(2):
                        ps_gu = ps_guP[tt]
                        bgu = b_gu[tt]
                        for dc in range(8):
                            OP("tensor", lambda e, ic=ic, tt=tt, dc=dc, ps_gu=ps_gu, xn2fm=xn2fm: e.matmul(
                                ps_gu[:, 0:512],
                                wg_sb[:, 1024 * dc + 128 * ic: 1024 * dc + 128 * (ic + 1)],
                                xn2fm[:, 1024 * dc + 512 * tt: 1024 * dc + 512 * (tt + 1)],
                                start=(dc == 0), stop=(dc == 7)),
                               reads=[b_wg, b_xn2[hf]], writes=[bgu])
                        for dc in range(8):
                            OP("tensor", lambda e, ic=ic, tt=tt, dc=dc, ps_gu=ps_gu, xn2fm=xn2fm: e.matmul(
                                ps_gu[:, 512:1024],
                                wu_sb[:, 1024 * dc + 128 * ic: 1024 * dc + 128 * (ic + 1)],
                                xn2fm[:, 1024 * dc + 512 * tt: 1024 * dc + 512 * (tt + 1)],
                                start=(dc == 0), stop=(dc == 7)),
                               reads=[b_wu, b_xn2[hf]], writes=[bgu])
                        OP("scalar", lambda e, ps_gu=ps_gu, tt=tt: e.activation(
                            out=sg_sbP[tt][:, :], in_=ps_gu[:, 0:512], func=AF.Silu),
                           reads=[bgu], writes=[b_sgP[tt]])
                        OP("vector", lambda e, ps_gu=ps_gu, tt=tt: e.tensor_tensor(
                            out=hp_sbP[tt][:, :], in0=sg_sbP[tt][:, :], in1=ps_gu[:, 512:1024],
                            op=ALU.mult),
                           reads=[b_sgP[tt], bgu], writes=[b_hpP[tt]])
                        OP("vector", lambda e, ic=ic, tt=tt: e.tensor_tensor(
                            out=h_all[:, 1024 * ic + 512 * tt: 1024 * ic + 512 * (tt + 1)],
                            in0=hp_sbP[tt][:, :], in1=wb_sb[:, 512 * tt:512 * (tt + 1)],
                            op=ALU.mult),
                           reads=[b_hpP[tt], b_wb], writes=[b_h])
                for tt in range(2):
                    for dc in range(8):
                        hd = dc % 2
                        for ic in range(8):
                            OP("tensor", lambda e, tt=tt, dc=dc, ic=ic, hd=hd: e.matmul(
                                ps_dn[:, 512 * hd:512 * (hd + 1)],
                                wd_sb[:, 1024 * ic + 128 * dc: 1024 * ic + 128 * (dc + 1)],
                                h_all[:, 1024 * ic + 512 * tt: 1024 * ic + 512 * (tt + 1)],
                                start=(ic == 0), stop=(ic == 7)),
                               reads=[b_wd, b_h], writes=[b_pdnH[hd]])
                        if ei == 0:
                            OP("vector", lambda e, tt=tt, dc=dc, hd=hd: e.tensor_copy(
                                out=acc[:, 1024 * dc + 512 * tt: 1024 * dc + 512 * (tt + 1)],
                                in_=ps_dn[:, 512 * hd:512 * (hd + 1)]),
                               reads=[b_pdnH[hd]], writes=[b_acc])
                        else:
                            OP("vector", lambda e, tt=tt, dc=dc, hd=hd: e.tensor_tensor(
                                out=acc[:, 1024 * dc + 512 * tt: 1024 * dc + 512 * (tt + 1)],
                                in0=acc[:, 1024 * dc + 512 * tt: 1024 * dc + 512 * (tt + 1)],
                                in1=ps_dn[:, 512 * hd:512 * (hd + 1)], op=ALU.add),
                               reads=[b_pdnH[hd], b_acc], writes=[b_acc])
            for tc in range(8):
                c = hf * 8 + tc
                for rnd in range(2):
                    for dcl in range(4):
                        dc = rnd * 4 + dcl
                        OP("tensor", lambda e, tc=tc, dc=dc, dcl=dcl: e.transpose(
                            ps_ft[:, 128 * dcl:128 * (dcl + 1)],
                            acc[:, 1024 * dc + 128 * tc: 1024 * dc + 128 * (tc + 1)],
                            identf[:, :]),
                           reads=[b_acc, b_const], writes=[b_pft])
                    OP("vector", lambda e, c=c, rnd=rnd: e.tensor_tensor(
                        out=out_st[:, :],
                        in0=x_bf[:, D * c + 512 * rnd: D * c + 512 * (rnd + 1)],
                        in1=ps_ft[:, :], op=ALU.add),
                       reads=[b_pft, b_x[c]], writes=[b_ost])
                    out_toks.append(dma(
                        "sync", out_h[128 * c:128 * (c + 1), 512 * rnd:512 * (rnd + 1)],
                        out_st[:, :], reads=[b_ost]))

    OP("gpsimd", lambda e: e.engine_nop(), deps=list(out_toks))
    pb.finalize_and_emit(nc, ctx)
    return nc


_NC_CACHE = {}


def _get_nc(debug=False):
    key = bool(debug)
    if key not in _NC_CACHE:
        _NC_CACHE[key] = build_program(debug=key)
    return _NC_CACHE[key]


def make_inmaps(x_full, wqkv, wout, wg, wu, wd, wr, temp):
    x_full = np.ascontiguousarray(x_full.reshape(NCORES * M, D).astype(np.float32))
    half = 32
    inv = 1.0 / (10000.0 ** (np.arange(half, dtype=np.float64) / half))
    masku = (np.arange(128)[:, None] <= np.arange(128)[None, :]).astype(np.float32)
    masku4 = np.tile(masku, (1, 4)).astype(ml_dtypes.bfloat16)
    esel4 = np.zeros((4, 256), ml_dtypes.bfloat16)
    for j in range(4):
        esel4[j, 64 * j:64 * (j + 1)] = 1.0
    e8sel = np.zeros((8, 1024), ml_dtypes.bfloat16)
    for j in range(8):
        e8sel[j, 128 * j:128 * (j + 1)] = 1.0
    identb = np.eye(128).astype(ml_dtypes.bfloat16)
    identf = np.eye(128, dtype=np.float32)
    onesb = np.ones((128, 1), ml_dtypes.bfloat16)
    temp1 = np.full((128, 1), float(temp), np.float32)
    consts = np.zeros((128, 8), np.float32)
    consts[:, 0] = 1.0 / D
    consts[:, 1] = 1e-5
    consts[:, 2] = SCALE
    consts[:, 3] = -10.0
    consts[:, 4] = -10.0 / 3.0
    in_maps = []
    for i in range(NCORES):
        pos = np.arange(i * M, (i + 1) * M, dtype=np.float64)[:, None] * inv[None, :]
        cosr = np.cos(pos).astype(np.float32).astype(ml_dtypes.bfloat16)
        sinr = np.sin(pos).astype(np.float32).astype(ml_dtypes.bfloat16)
        prefmask = np.zeros((64, 8), np.float32)
        prefmask[:, :i] = 1.0
        in_maps.append({
            "x": x_full[i * M:(i + 1) * M],
            "wqkv": wqkv, "wout": wout, "wg": wg, "wu": wu, "wd": wd, "wr": wr,
            "temp1": temp1, "cosr": cosr, "sinr": sinr,
            "identb": identb, "identf": identf, "masku4": masku4,
            "onesb": onesb, "esel4": esel4, "e8sel": e8sel,
            "prefmask": prefmask, "consts": consts,
        })
    return in_maps


def kernel(L0_attn_x, L0_attn_w_qkv, L0_attn_w_out, L0_ffn_Wg, L0_ffn_Wu,
           L0_ffn_Wd, L0_ffn_Wr, L0_ffn_temperature):
    x = np.asarray(L0_attn_x)
    in_maps = make_inmaps(x, np.asarray(L0_attn_w_qkv, np.float32),
                          np.asarray(L0_attn_w_out, np.float32),
                          np.asarray(L0_ffn_Wg, np.float32),
                          np.asarray(L0_ffn_Wu, np.float32),
                          np.asarray(L0_ffn_Wd, np.float32),
                          np.asarray(L0_ffn_Wr, np.float32),
                          float(np.asarray(L0_ffn_temperature).reshape(-1)[0]))
    nc = _get_nc(debug=False)
    res = run_bass_kernel_spmd(nc, in_maps, core_ids=list(range(NCORES)))
    out = np.concatenate([r["out"] for r in res.results], axis=0)
    return out.reshape(1, NCORES * M, D).astype(np.float32)
